# version 05 (final)
# baseline (speedup 1.0000x reference)
import numpy as np
import ml_dtypes
import concourse.bass as bass
import concourse.mybir as mybir
from concourse.bass_utils import run_bass_kernel_spmd

F32 = mybir.dt.float32
BF16 = mybir.dt.bfloat16
I32 = mybir.dt.int32
AF = mybir.ActivationFunctionType
ALU = mybir.AluOpType
BF = ml_dtypes.bfloat16

B, T, C = 2, 4096, 2048
H, HD = 32, 64
NT = 1024
Q = 256
NCH = NT // Q
PREF = 3072
SBLK = 1024
NSB = PREF // SBLK
CCn = 16
FI = 8192
LN_EPS = 1e-5
GN_EPS = 1e-5 * 144
SBUF_BASE = 16512
USABLE = 212832

DEBUG_TAPS = []
PHASE_LIMIT = None
Y_STAGE = 7


class _PhaseStop(Exception):
    pass


def _consts():
    c = {}
    c["ident"] = np.eye(128, dtype=BF)
    c["ident32"] = np.eye(32, dtype=np.float32)
    c["ones_c"] = np.ones((128, 1), BF)
    c["ones_rb"] = np.ones((1, 128), BF)
    c["ones_rf"] = np.ones((1, 128), np.float32)
    hp = np.zeros((32, 128), np.float32)
    for h in range(32):
        hp[h, (h % 2) * 64:(h % 2) * 64 + 64] = 1.0
    c["HP"] = hp
    hc = np.zeros((32, 16), np.float32)
    for h in range(32):
        hc[h, h // 2] = 1.0
    c["HC"] = hc
    c["QMODB"] = np.broadcast_to((np.arange(1024) % Q).astype(np.float32),
                                 (128, 1024)).copy()
    c["REVQ"] = np.stack([255.0 - np.arange(128), 127.0 - np.arange(128)],
                         axis=1).astype(np.float32)
    dpos, mlow, diag = [], [], []
    for half in range(2):
        qp = half * 128 + np.arange(128)[:, None]
        qq = np.arange(256)[None, :]
        dpos.append(np.maximum(qq - qp - 1, 0).astype(np.float32))
        mlow.append((qq > qp).astype(np.float32))
        diag.append((qq == qp).astype(np.float32))
    c["DPOS"] = np.concatenate(dpos, axis=1)
    c["MLOW"] = np.concatenate(mlow, axis=1)
    c["DIAG"] = np.concatenate(diag, axis=1)
    c["E32"] = np.repeat(np.eye(32, dtype=np.float32), 64, axis=1)
    cv = np.array([0.0, 1.0, -1.0, -0.5, 256.0, LN_EPS, GN_EPS, -128.0],
                  np.float32)
    c["CVALS"] = np.broadcast_to(cv, (128, 8)).copy()
    return c


class Prog:
    ENGS = ("pe", "dve", "act", "gp", "sp")

    def __init__(self, nc):
        self.nc = nc
        self.stream = {e: [] for e in self.ENGS}
        self.sems = {}
        self.cnt = {}
        self._alloc = []
        self.tensors = {}
        self.tap_names = []

    def sem(self, name):
        h = self.nc.semaphore(name).__enter__()
        self.sems[name] = h
        self.cnt[name] = 0

    def inc(self, name, n=1):
        self.cnt[name] += n
        return self.cnt[name]

    def val(self, name):
        return self.cnt[name]

    def emit(self, eng, fn):
        self.stream[eng].append(fn)

    def wait(self, eng, sname, v):
        if v <= 0:
            return
        h = self.sems[sname]
        self.emit(eng, lambda e, h=h, v=v: e.wait_ge(h, v))

    def op(self, eng, fn, sig=None, inc=1):
        if sig is None:
            self.emit(eng, fn)
            return None
        v = self.inc(sig, inc)
        h = self.sems[sig]
        self.emit(eng, lambda e, fn=fn, h=h, n=inc: fn(e).then_inc(h, n))
        return v

    def selfwait(self, eng):
        sem = {"dve": "s_dve", "act": "s_act", "pe": "s_pe"}[eng]
        self.wait(eng, sem, self.val(sem))

    def dma(self, eng, dst, src, sig):
        return self.op(eng, lambda e, d=dst, s=src: e.dma_start(d, s),
                       sig=sig, inc=16)

    def barrier(self):
        own = {"pe": ["s_pe"], "dve": ["s_dve"], "act": ["s_act"],
               "gp": ["s_w"], "sp": ["s_in", "s_out"]}
        for e in self.ENGS:
            for s in own[e]:
                self.wait(e, s, self.val(s))
        tgt = {}
        for e in self.ENGS:
            tgt[e] = self.inc("b_" + e, 1)
            h = self.sems["b_" + e]
            self.emit(e, lambda eng, h=h: eng.nop().then_inc(h, 1))
        for e in self.ENGS:
            for o in self.ENGS:
                if o != e:
                    self.wait(e, "b_" + o, tgt[o])

    def buf(self, name, shape, dtype, s_phase, e_phase, align=64, at=None):
        size = int(np.prod(shape[1:])) * mybir.dt.size(dtype)
        size = (size + align - 1) // align * align
        if at is not None:
            off = at
        else:
            spans = sorted((o, en) for (o, en, s, e) in self._alloc
                           if not (e < s_phase or s > e_phase))
            off = 0
            for (o, en) in spans:
                if off + size <= o:
                    break
                off = max(off, en)
            assert off + size <= USABLE, ("sbuf overflow", name, off, size)
            self._alloc.append((off, off + size, s_phase, e_phase))
        t = self.nc.alloc_sbuf_tensor_at(name, shape, dtype,
                                         offset=SBUF_BASE + off)
        self.offs = getattr(self, "offs", {})
        self.offs[name] = off
        self.tensors[name] = t
        return t

    def tap(self, name, ap):
        if name not in DEBUG_TAPS:
            return
        out = self.nc.dram_tensor("tap_" + name, list(ap.shape), ap.dtype,
                                  kind="ExternalOutput").ap()
        for s in ("s_pe", "s_dve", "s_act"):
            self.wait("sp", s, self.val(s))
        v = self.dma("sp", out, ap, "s_out")
        self.wait("sp", "s_out", v)
        self.tap_names.append(name)


def build_program():
    nc = bass.Bass("TRN2", target_bir_lowering=False)
    P = Prog(nc)
    CON = _consts()

    def din(name, shape, dtype):
        return nc.dram_tensor(name, shape, dtype, kind="ExternalInput").ap()

    xT = din("xT", [C, NT], F32)
    xP = din("xP", [C, PREF], F32)
    wT = {n: din("w" + n + "T", [C, C], F32) for n in "rkvgo"}
    gateT = din("gateT", [C, FI], I32)
    upT = din("upT", [C, FI], I32)
    downT = din("downT", [FI, C], I32)
    maa = {m: din("maa_" + m, [128, 16], F32) for m in "kvrg"}
    gamma_pp = din("gamma_pp", [128, 16], F32)
    beta_pp = din("beta_pp", [128, 16], F32)
    gs_pp = din("gs_pp", [128, 64], F32)
    us_pp = din("us_pp", [128, 64], F32)
    ds_pp = din("ds_pp", [128, 16], F32)
    decay = din("decay", [32, 1], F32)
    faaaa = din("faaaa", [32, 1], F32)
    out_hbm = nc.dram_tensor("out_fm", [C, NT], F32, kind="ExternalOutput").ap()

    conh = {k: nc.inline_tensor(np.ascontiguousarray(v), name="c_" + k)
            for k, v in CON.items()}

    for s in ("s_pe", "s_dve", "s_act", "s_w", "s_in", "s_out"):
        P.sem(s)
    for e in P.ENGS:
        P.sem("b_" + e)

    (PH_PRO, PH_PRE, PH_LN1, PH_MIX1, PH_KP, PH_VP, PH_SC, PH_MIX2, PH_RP,
     PH_GP, PH_Y, PH_WO, PH_LN2, PH_FFN, PH_OUT) = range(15)

    ident = P.buf("ident", [128, 128], BF16, PH_PRO, PH_Y)
    ones_c = P.buf("ones_c", [128, 1], BF16, PH_PRO, PH_FFN)
    ones_rb = P.buf("ones_rb", [1, 128], BF16, PH_PRO, PH_FFN)
    ones_rf = P.buf("ones_rf", [1, 128], F32, PH_PRO, PH_FFN)
    sb_maa = {m: P.buf("maa" + m, [128, 16], F32, PH_PRO, PH_MIX2)
              for m in "kvrg"}
    sb_gamma = P.buf("gammapp", [128, 16], F32, PH_PRO, PH_Y)
    sb_beta = P.buf("betapp", [128, 16], F32, PH_PRO, PH_Y)
    sb_gs = P.buf("gspp", [128, 64], F32, PH_PRO, PH_FFN)
    sb_us = P.buf("uspp", [128, 64], F32, PH_PRO, PH_FFN)
    sb_ds = P.buf("dspp", [128, 16], F32, PH_PRO, PH_FFN)
    sb_decay = P.buf("decay", [32, 1], F32, PH_PRO, PH_PRO)
    sb_u32 = P.buf("u32", [32, 1], F32, PH_PRO, PH_PRO)
    sb_lnw = P.buf("lnw", [32, 1], F32, PH_PRO, PH_PRO)
    sb_id32 = P.buf("id32", [32, 32], F32, PH_PRO, PH_PRO)
    sb_hp = P.buf("HP", [32, 128], F32, PH_PRO, PH_PRO)
    sb_hc = P.buf("HC", [32, 16], F32, PH_PRO, PH_PRO)
    sb_revq = P.buf("REVQ", [128, 2], F32, PH_PRO, PH_PRO)
    sb_scr32 = P.buf("scr32", [64, 32], F32, PH_PRO, PH_PRO)
    sb_scr32b = P.buf("scr32b", [1, 32], F32, PH_PRO, PH_PRO)
    sb_e32 = P.buf("E32", [32, 2048], F32, PH_PRO, PH_PRO)
    sb_lnwpp = P.buf("lnwpp", [128, 16], F32, PH_PRO, PH_Y)
    sb_wspp = P.buf("wspp", [128, 16], F32, PH_PRO, PH_SC)
    sb_cvals = P.buf("cvals", [128, 8], F32, PH_PRO, PH_OUT)
    for _i, _v in enumerate([0.0, 1.0, -1.0, -0.5, 256.0, LN_EPS, GN_EPS,
                             -128.0]):
        nc.const_aps.register(F32, float(_v), sb_cvals[:, _i:_i + 1])
    sb_lnwhb = P.buf("lnwhb", [128, 32], F32, PH_PRO, PH_Y)
    sb_uhb = P.buf("uhb", [128, 32], F32, PH_PRO, PH_Y)
    sb_wk = P.buf("wksm", [128, 64], F32, PH_PRO, PH_SC)
    sb_xnprev = P.buf("xnprev", [128, 16], BF16, PH_PRO, PH_MIX2)
    state_sb = P.buf("state", [128, 1024], F32, PH_PRO, PH_SC)

    xn = P.buf("xn", [128, CCn * 1024], BF16, PH_LN1, PH_GP)
    k_fm = P.buf("k_fm", [128, CCn * 1024], BF16, PH_KP, PH_Y)
    v_tok = P.buf("v_tok", [128, 8 * 2048], BF16, PH_VP, PH_Y)
    hist = P.buf("hist", [128, NCH * 1024], BF16, PH_SC, PH_Y)
    r_fm = P.buf("r_fm", [128, CCn * 1024], BF16, PH_RP, PH_Y)
    g_fm = P.buf("g_fm", [128, CCn * 1024], BF16, PH_GP, PH_Y)
    z_fm = P.buf("z_fm", [128, CCn * 1024], BF16, PH_Y, PH_WO)
    x2 = P.buf("x2", [128, CCn * 1024], F32, PH_WO, PH_OUT)
    h_fm = P.buf("h_fm", [128, CCn * 1024], BF16, PH_LN2, PH_FFN)
    f_sb = P.buf("f", [128, CCn * 1024], BF16, PH_FFN, PH_FFN)
    xx_xk = P.buf("xx_xk", [128, CCn * 1024], BF16, PH_MIX1, PH_KP)
    xv = P.buf("xv", [128, CCn * 1024], BF16, PH_MIX1, PH_VP)
    xr = P.buf("xr", [128, CCn * 1024], BF16, PH_MIX2, PH_RP)
    xnP = P.buf("xnP", [128, CCn * 1024], BF16, PH_PRE, PH_PRE)
    xkP = P.buf("xkP", [128, CCn * 1024], BF16, PH_PRE, PH_PRE)
    xvP = P.buf("xvP", [128, CCn * 1024], BF16, PH_PRE, PH_PRE)
    k_fmP = P.buf("k_fmP", [128, CCn * 1024], BF16, PH_PRE, PH_PRE)
    v_tokP = P.buf("v_tokP", [128, 8 * 2048], BF16, PH_PRE, PH_PRE)
    ringP = P.buf("ringP", [128, 2 * 2048], BF16, PH_PRE, PH_PRE)
    vfmP = P.buf("vfmP", [128, 1024], BF16, PH_PRE, PH_PRE)
    kwP = P.buf("kwP", [128, 2 * 2048], BF16, PH_PRE, PH_PRE,
                at=P.offs["ringP"])
    x_bf = P.buf("x_bf", [128, CCn * 1024], BF16, PH_LN1, PH_LN1)
    xq = P.buf("xq", [128, CCn * 256], F32, PH_LN1, PH_LN1)
    mu_b = P.buf("mu_b", [128, 1024], F32, PH_PRE, PH_LN1)
    rstd_b = P.buf("rstd_b", [128, 1024], F32, PH_PRE, PH_LN1)
    rows1 = P.buf("rows1", [1, 2048], F32, PH_PRE, PH_LN1)
    rowscr = P.buf("rowscr", [1, 1024], F32, PH_PRE, PH_LN1)
    sqscr = P.buf("sqscr", [128, 2 * 1024], BF16, PH_PRE, PH_LN1)
    ring = P.buf("ring", [128, 2 * 2048], BF16, PH_KP, PH_GP)
    vfm_tmp = P.buf("vfm", [128, 1024], BF16, PH_VP, PH_VP)
    kw = P.buf("kw", [128, 2 * 2048], BF16, PH_SC, PH_SC)
    xx_scr = P.buf("xxscr", [128, 1024], BF16, PH_MIX2, PH_MIX2)
    wmat = P.buf("wmat", [128, 4 * 512], BF16, PH_Y, PH_Y)
    rwb = P.buf("rwb", [128, 2 * 1024], BF16, PH_Y, PH_Y)
    wbtmp = P.buf("wbtmp", [128, 1024], F32, PH_Y, PH_Y)
    ynst = P.buf("ynst", [128, 2 * 256], BF16, PH_Y, PH_Y)
    am = P.buf("am", [128, 4 * 512], BF16, PH_Y, PH_Y)
    stat8 = P.buf("stat8", [128, 2 * 20], F32, PH_Y, PH_Y)
    ytile = P.buf("ytile", [128, 2 * 512], F32, PH_Y, PH_Y)
    sb_dpos = P.buf("DPOS", [128, 512], F32, PH_Y, PH_Y)
    sb_mlow = P.buf("MLOW", [128, 512], F32, PH_Y, PH_Y)
    sb_diag = P.buf("DIAG", [128, 512], F32, PH_Y, PH_Y)
    sb_qmodb = P.buf("QMODB", [128, 1024], F32, PH_Y, PH_Y)
    wmtmp = P.buf("wmtmp", [128, 512], F32, PH_Y, PH_Y)
    ring2 = P.buf("ring2", [128, 2 * 2048], BF16, PH_WO, PH_WO)
    x2_bf = P.buf("x2bf", [128, CCn * 1024], BF16, PH_LN2, PH_LN2)
    mu2_b = P.buf("mu2b", [128, 1024], F32, PH_LN2, PH_LN2)
    rstd2_b = P.buf("rstd2b", [128, 1024], F32, PH_LN2, PH_LN2)
    rows2 = P.buf("rows2", [1, 2048], F32, PH_LN2, PH_FFN)
    rowscr2 = P.buf("rowscr2", [1, 1024], F32, PH_LN2, PH_FFN)
    sqscr2 = P.buf("sqscr2", [128, 2 * 1024], BF16, PH_LN2, PH_LN2)
    ring_gu = P.buf("ring_gu", [128, 4 * 2048], BF16, PH_FFN, PH_FFN)
    ring_dn = P.buf("ring_dn", [128, 4 * 2048], BF16, PH_FFN, PH_FFN)
    cb_rows = P.buf("cbrows", [1, 2048], F32, PH_LN2, PH_FFN)
    cneg_bf = P.buf("cnegbf", [1, 3 * 1024], BF16, PH_LN2, PH_FFN)
    sg_tmp = P.buf("sgtmp", [128, 1024], F32, PH_FFN, PH_FFN)
    f0_tmp = P.buf("f0tmp", [128, 1024], F32, PH_FFN, PH_FFN)

    ps_big0 = nc.alloc_psum_tensor("ps_big0", [128, 1024], F32)
    ps_big1 = nc.alloc_psum_tensor("ps_big1", [128, 1024], F32)
    ps_state = nc.alloc_psum_tensor("ps_state", [128, 1024], F32)
    ps_tr = nc.alloc_psum_tensor("ps_tr", [128, 1024], BF16)
    ps_sml = nc.alloc_psum_tensor("ps_sml", [128, 512], F32)

    MM = lambda o, l, r, st, sp, tp=None: (
        lambda e, o=o, l=l, r=r, st=st, sp=sp, tp=tp:
        e.matmul(o, l, r, start=st, stop=sp, tile_position=tp))

    def ckpt(ph):
        if PHASE_LIMIT is not None and ph > PHASE_LIMIT:
            raise _PhaseStop()

    try:
        for dst, src in [
            (ident[:], conh["ident"].ap()), (ones_c[:], conh["ones_c"].ap()),
            (ones_rb[:], conh["ones_rb"].ap()), (ones_rf[:], conh["ones_rf"].ap()),
            (sb_id32[:], conh["ident32"].ap()),
            (sb_hp[:], conh["HP"].ap()), (sb_hc[:], conh["HC"].ap()),
            (sb_revq[:], conh["REVQ"].ap()), (sb_e32[:], conh["E32"].ap()),
            (sb_maa["k"][:], maa["k"]), (sb_maa["v"][:], maa["v"]),
            (sb_maa["r"][:], maa["r"]), (sb_maa["g"][:], maa["g"]),
            (sb_gamma[:], gamma_pp), (sb_beta[:], beta_pp),
            (sb_gs[:], gs_pp), (sb_us[:], us_pp), (sb_ds[:], ds_pp),
            (sb_decay[:], decay), (sb_u32[:], faaaa),
            (sb_cvals[:], conh["CVALS"].ap()),
        ]:
            P.dma("sp", dst, src, "s_in")
        for e in ("act", "dve", "pe"):
            P.wait(e, "s_in", P.val("s_in"))

        v = P.op("act", lambda e: e.activation(sb_lnw[:], sb_decay[:], AF.Exp),
                 sig="s_act")
        P.wait("dve", "s_act", v)
        v = P.op("dve", lambda e: e.tensor_scalar_mul(sb_lnw[:], sb_lnw[:], -1.0),
                 sig="s_dve")
        P.wait("pe", "s_dve", v)

        v = P.op("pe", MM(ps_sml[0:1, 0:32], sb_lnw[:], sb_id32[:], True, True),
                 sig="s_pe")
        v = P.op("pe", MM(ps_sml[0:1, 32:64], sb_u32[:], sb_id32[:], True, True),
                 sig="s_pe")
        P.wait("dve", "s_pe", v)
        v = P.op("dve", lambda e: e.tensor_copy(sb_scr32[0:1, 0:32],
                                                ps_sml[0:1, 0:32]), sig="s_dve")
        v = P.op("dve", lambda e: e.tensor_copy(sb_scr32b[0:1, 0:32],
                                                ps_sml[0:1, 32:64]), sig="s_dve")
        P.wait("pe", "s_dve", v)
        v = P.op("pe", MM(ps_sml[:, 0:32], ones_rf[:], sb_scr32[0:1, 0:32],
                          True, True), sig="s_pe")
        v = P.op("pe", MM(ps_sml[:, 32:64], ones_rf[:], sb_scr32b[0:1, 0:32],
                          True, True), sig="s_pe")
        P.wait("dve", "s_pe", v)
        v = P.op("dve", lambda e: e.tensor_copy(sb_lnwhb[:], ps_sml[:, 0:32]),
                 sig="s_dve")
        v = P.op("dve", lambda e: e.tensor_copy(sb_uhb[:], ps_sml[:, 32:64]),
                 sig="s_dve")
        P.wait("act", "s_dve", v)
        P.wait("pe", "s_dve", v)
        for half in range(2):
            v = P.op("dve", lambda e, h=half: e.tensor_single_scalar(
                sb_wk[:, h * 32:h * 32 + 32], sb_lnwhb[:], sb_revq[:, h:h + 1],
                ALU.mult), sig="s_dve")
            P.wait("act", "s_dve", v)
            P.op("act", lambda e, h=half: e.activation(
                sb_wk[:, h * 32:h * 32 + 32], sb_wk[:, h * 32:h * 32 + 32],
                AF.Exp), sig="s_act")

        def make_pp(dst, src_col):
            v = P.op("dve", lambda e, s=src_col: e.tensor_single_scalar(
                sb_scr32[0:32, 16:32], sb_hc[:], s[:], ALU.mult), sig="s_dve")
            P.wait("pe", "s_dve", v)
            v = P.op("pe", MM(ps_sml[:, 64:80], sb_hp[:], sb_scr32[0:32, 16:32],
                              True, True), sig="s_pe")
            P.wait("dve", "s_pe", v)
            v = P.op("dve", lambda e, d=dst: e.tensor_copy(d[:], ps_sml[:, 64:80]),
                     sig="s_dve")
            P.wait("pe", "s_dve", v)
            return v

        make_pp(sb_lnwpp, sb_lnw)
        v = make_pp(sb_wspp, sb_lnw)
        P.wait("act", "s_dve", v)
        P.op("act", lambda e: e.activation(sb_wspp[:], sb_wspp[:], AF.Exp,
                                           scale=float(Q)), sig="s_act")
        P.op("dve", lambda e: e.memset(state_sb[:], 0.0), sig="s_dve")
        P.op("dve", lambda e: e.memset(sb_xnprev[:], 0.0), sig="s_dve")
        P.barrier()

        def emit_ln(raw_bf, mu, rstd, row_t, rsc, sq, eps):
            pe_after = {}
            for cc in range(CCn):
                pp = cc % 2
                if cc >= 2:
                    P.wait("act", "s_pe", pe_after[cc - 2])
                v = P.op("act", lambda e, cc=cc, pp=pp: e.activation(
                    sq[:, pp * 1024:(pp + 1) * 1024],
                    raw_bf[:, cc * 1024:(cc + 1) * 1024], AF.Square), sig="s_act")
                P.wait("pe", "s_act", v)
                for hh in range(2):
                    P.op("pe", MM(ps_big0[0:1, hh * 512:(hh + 1) * 512], ones_c[:],
                                  raw_bf[:, cc * 1024 + hh * 512:
                                         cc * 1024 + (hh + 1) * 512],
                                  cc == 0, cc == CCn - 1), sig="s_pe")
                    vv = P.op("pe", MM(ps_big1[0:1, hh * 512:(hh + 1) * 512],
                                       ones_c[:],
                                       sq[:, pp * 1024 + hh * 512:
                                          pp * 1024 + (hh + 1) * 512],
                                       cc == 0, cc == CCn - 1), sig="s_pe")
                pe_after[cc] = vv
            P.wait("dve", "s_pe", P.val("s_pe"))
            inv = 1.0 / C
            v = P.op("dve", lambda e: e.tensor_single_scalar(
                row_t[0:1, 0:1024], ps_big0[0:1, 0:1024], inv, ALU.mult),
                sig="s_dve")
            v = P.op("dve", lambda e: e.tensor_mul(
                rsc[0:1, 0:1024], row_t[0:1, 0:1024], row_t[0:1, 0:1024]),
                sig="s_dve")
            v = P.op("dve", lambda e: e.scalar_tensor_tensor(
                row_t[0:1, 1024:2048], ps_big1[0:1, 0:1024], inv,
                rsc[0:1, 0:1024], ALU.mult, ALU.subtract), sig="s_dve")
            P.wait("act", "s_dve", v)
            v = P.op("act", lambda e: e.activation(
                rsc[0:1, 0:1024], row_t[0:1, 1024:2048], AF.Ln, bias=float(eps)),
                sig="s_act")
            v = P.op("act", lambda e: e.activation(
                row_t[0:1, 1024:2048], rsc[0:1, 0:1024], AF.Exp, scale=-0.5),
                sig="s_act")
            P.wait("pe", "s_act", v)
            P.wait("pe", "s_dve", P.val("s_dve"))
            for hh in range(2):
                P.op("pe", MM(ps_big0[:, hh * 512:(hh + 1) * 512], ones_rf[:],
                              row_t[0:1, hh * 512:(hh + 1) * 512], True, True),
                     sig="s_pe")
                v = P.op("pe", MM(ps_big1[:, hh * 512:(hh + 1) * 512], ones_rf[:],
                                  row_t[0:1, 1024 + hh * 512:1024 + (hh + 1) * 512],
                                  True, True), sig="s_pe")
            P.wait("dve", "s_pe", v)
            v = P.op("dve", lambda e: e.tensor_copy(mu[:], ps_big0[:, 0:1024]),
                     sig="s_dve")
            v = P.op("dve", lambda e: e.tensor_copy(rstd[:], ps_big1[:, 0:1024]),
                     sig="s_dve")
            P.wait("pe", "s_dve", v)
            return v

        def emit_mix(dst, xxbuf, cc, mcol, xnb, inplace_xx=False):
            base = cc * 1024
            if not inplace_xx:
                P.op("dve", lambda e: e.tensor_sub(
                    xxbuf[:, 1:1024], xnb[:, base:base + 1023],
                    xnb[:, base + 1:base + 1024]), sig="s_dve")
                P.op("dve", lambda e: e.tensor_sub(
                    xxbuf[:, 0:1], sb_xnprev[:, cc:cc + 1],
                    xnb[:, base:base + 1]), sig="s_dve")
            return P.op("dve", lambda e: e.scalar_tensor_tensor(
                dst[:, 0:1024] if dst.shape[1] == 1024 else dst[:, base:base + 1024],
                xxbuf[:, 0:1024], mcol, xnb[:, base:base + 1024],
                ALU.mult, ALU.add), sig="s_dve")

        def emit_proj(wdram, rhs_view, evac, ringb, n_ot=CCn, wait_rhs=None):
            wsrc = wdram.rearrange("(cc p) o -> p cc o", p=128)
            evac_done = {}
            mm_last = {}
            prev_pe = P.val("s_pe")
            for ot in range(n_ot):
                slot = ot % 2
                ps = ps_big0 if ot % 2 == 0 else ps_big1
                P.wait("gp", "s_pe", mm_last[ot - 2] if ot >= 2 else prev_pe)
                vw = P.dma(
                    "gp", ringb[:].rearrange("p (s cc o) -> p s cc o", s=2, o=128)
                    [:, slot], wsrc[:, :, ot * 128:(ot + 1) * 128], "s_w")
                P.wait("pe", "s_w", vw)
                if ot >= 2:
                    eng, vv = evac_done[ot - 2]
                    P.wait("pe", "s_" + eng, vv)
                if wait_rhs is not None and ot == 0:
                    P.wait("pe", wait_rhs[0], wait_rhs[1])
                for cc in range(CCn):
                    for hh in range(2):
                        v = P.op("pe", MM(
                            ps[:, hh * 512:(hh + 1) * 512],
                            ringb[:, slot * 2048 + cc * 128:
                                  slot * 2048 + (cc + 1) * 128],
                            rhs_view(cc, hh), cc == 0, cc == CCn - 1), sig="s_pe")
                mm_last[ot] = v
                evac_done[ot] = evac(ot, ps, v)
            return mm_last, evac_done

        def emit_ascan(kbuf, vbuf, nch, hist_buf, kwb):
            for c in range(nch):
                if hist_buf is not None:
                    v = P.op("dve", lambda e, c=c: e.tensor_copy(
                        hist_buf[:, c * 1024:(c + 1) * 1024], state_sb[:]),
                        sig="s_dve")
                for half in range(2):
                    for grp in range(2):
                        P.wait("pe", "s_dve", P.val("s_dve"))
                        for cc8 in range(8):
                            cc = grp * 8 + cc8
                            v = P.op("pe", lambda e, cc=cc, cc8=cc8, half=half, c=c:
                                     e.transpose(
                                         ps_tr[:, cc8 * 128:(cc8 + 1) * 128],
                                         kbuf[:, cc * 1024 + c * 256 + half * 128:
                                              cc * 1024 + c * 256 + half * 128 + 128],
                                         ident[:]), sig="s_pe")
                        P.wait("dve", "s_pe", v)
                        wk_ap = bass.AP(sb_wk, half * 32 + grp * 16,
                                        [[64, 128], [1, 16], [0, 64]])
                        v = P.op("dve", lambda e, half=half, grp=grp,
                                 wk_ap=wk_ap, kwb=kwb: e.tensor_tensor(
                                     kwb[:, half * 2048 + grp * 1024:
                                         half * 2048 + (grp + 1) * 1024]
                                     .rearrange("p (hh r) -> p hh r", r=64),
                                     ps_tr[:].rearrange("p (hh r) -> p hh r", r=64),
                                     wk_ap, ALU.mult), sig="s_dve")
                P.wait("pe", "s_dve", P.val("s_dve"))
                dve_state = {}
                for h in range(H):
                    rb = (h % 2) * 64
                    slot = (h % 2) * 512 + ((h // 2) % 2) * 128
                    if h >= 2:
                        P.wait("pe", "s_dve", dve_state[h - 2])
                    for ts in range(2):
                        v = P.op("pe", MM(
                            ps_big1[rb:rb + 64, slot:slot + 64],
                            kwb[:, ts * 2048 + h * 64:ts * 2048 + (h + 1) * 64],
                            vbuf[:, (c * 2 + ts) * 2048 + h * 64:
                                 (c * 2 + ts) * 2048 + (h + 1) * 64],
                            ts == 0, ts == 1, (0, rb)), sig="s_pe")
                    P.wait("dve", "s_pe", v)
                    dve_state[h] = P.op("dve", lambda e, h=h, rb=rb, slot=slot:
                                        e.scalar_tensor_tensor(
                        state_sb[rb:rb + 64, (h // 2) * 64:(h // 2) * 64 + 64],
                        state_sb[rb:rb + 64, (h // 2) * 64:(h // 2) * 64 + 64],
                        sb_wspp[rb:rb + 64, h // 2:h // 2 + 1],
                        ps_big1[rb:rb + 64, slot:slot + 64],
                        ALU.mult, ALU.add), sig="s_dve")
                P.wait("pe", "s_dve", P.val("s_dve"))

        def emit_vproj_evac(vfm_b, vtok_b):
            def evac(ot, ps, mmv):
                P.wait("dve", "s_pe", mmv)
                v1 = P.op("dve", lambda e, ps=ps: e.tensor_copy(
                    vfm_b[:], ps[:, 0:1024]), sig="s_dve")
                P.wait("pe", "s_dve", v1)
                for j in range(8):
                    vp = P.op("pe", lambda e, j=j: e.transpose(
                        ps_tr[:, j * 128:(j + 1) * 128],
                        vfm_b[:, j * 128:(j + 1) * 128], ident[:]), sig="s_pe")
                P.wait("dve", "s_pe", vp)
                v2 = P.op("dve", lambda e, ot=ot: e.tensor_copy(
                    vtok_b[:].rearrange("p (tt ch) -> p tt ch", ch=2048)
                    [:, :, ot * 128:(ot + 1) * 128],
                    ps_tr[:].rearrange("p (tt ch) -> p tt ch", ch=128)),
                    sig="s_dve")
                return ("dve", v2)
            return evac

        ckpt(PH_PRE)
        xPv = xP.rearrange("(cc p) t -> p cc t", p=128)
        for sb in range(NSB):
            P.wait("gp", "s_dve", P.val("s_dve"))
            vin = P.dma("gp", xnP[:].rearrange("p (cc t) -> p cc t", t=1024),
                        xPv[:, :, sb * 1024:(sb + 1) * 1024], "s_w")
            for e in ("pe", "act", "dve"):
                P.wait(e, "s_w", vin)
            emit_ln(xnP, mu_b, rstd_b, rows1, rowscr, sqscr, LN_EPS)
            for cc in range(CCn):
                P.op("dve", lambda e, cc=cc: e.tensor_sub(
                    xnP[:, cc * 1024:(cc + 1) * 1024],
                    xnP[:, cc * 1024:(cc + 1) * 1024], mu_b[:]), sig="s_dve")
                v = P.op("dve", lambda e, cc=cc: e.tensor_mul(
                    xnP[:, cc * 1024:(cc + 1) * 1024],
                    xnP[:, cc * 1024:(cc + 1) * 1024], rstd_b[:]), sig="s_dve")
            for cc in range(CCn):
                emit_mix(xvP, xkP[:, cc * 1024:(cc + 1) * 1024], cc,
                         sb_maa["v"][:, cc:cc + 1], xnP)
                v = emit_mix(xkP, xkP[:, cc * 1024:(cc + 1) * 1024], cc,
                             sb_maa["k"][:, cc:cc + 1], xnP, inplace_xx=True)
            v = P.op("dve", lambda e: e.tensor_copy(
                sb_xnprev[:], xnP[:].rearrange("p (cc t) -> p cc t", t=1024)
                [:, :, 1023]), sig="s_dve")
            mix_done = v
            def kevacP(ot, ps, mmv):
                P.wait("dve", "s_pe", mmv)
                v = P.op("dve", lambda e, ot=ot, ps=ps: e.tensor_copy(
                    k_fmP[:, ot * 1024:(ot + 1) * 1024], ps[:, 0:1024]),
                    sig="s_dve")
                return ("dve", v)
            emit_proj(wT["k"], lambda cc, hh: xkP[:, cc * 1024 + hh * 512:
                                                  cc * 1024 + (hh + 1) * 512],
                      kevacP, ringP, wait_rhs=("s_dve", mix_done))
            emit_proj(wT["v"], lambda cc, hh: xvP[:, cc * 1024 + hh * 512:
                                                  cc * 1024 + (hh + 1) * 512],
                      emit_vproj_evac(vfmP, v_tokP), ringP)
            emit_ascan(k_fmP, v_tokP, SBLK // Q, None, kwP)
        P.barrier()

        ckpt(PH_LN1)
        xTv = xT.rearrange("(cc p) t -> p cc t", p=128)
        P.dma("gp", x_bf[:].rearrange("p (cc t) -> p cc t", t=1024), xTv, "s_w")
        for e in ("pe", "act", "dve"):
            P.wait(e, "s_w", P.val("s_w"))
        emit_ln(x_bf, mu_b, rstd_b, rows1, rowscr, sqscr, LN_EPS)
        for qq in range(4):
            vq = P.dma("sp", xq[:].rearrange("p (cc t) -> p cc t", t=256),
                       xTv[:, :, qq * 256:(qq + 1) * 256], "s_in")
            P.wait("dve", "s_in", vq)
            for cc in range(CCn):
                P.op("dve", lambda e, cc=cc, qq=qq: e.tensor_sub(
                    xn[:, cc * 1024 + qq * 256:cc * 1024 + (qq + 1) * 256],
                    xq[:, cc * 256:(cc + 1) * 256],
                    mu_b[:, qq * 256:(qq + 1) * 256]), sig="s_dve")
                v = P.op("dve", lambda e, cc=cc, qq=qq: e.tensor_mul(
                    xn[:, cc * 1024 + qq * 256:cc * 1024 + (qq + 1) * 256],
                    xn[:, cc * 1024 + qq * 256:cc * 1024 + (qq + 1) * 256],
                    rstd_b[:, qq * 256:(qq + 1) * 256]), sig="s_dve")
            if qq < 3:
                P.wait("sp", "s_dve", v)
        P.tap("xn", xn[:])
        P.barrier()

        ckpt(PH_MIX1)
        for cc in range(CCn):
            emit_mix(xv, xx_xk[:, cc * 1024:(cc + 1) * 1024], cc,
                     sb_maa["v"][:, cc:cc + 1], xn)
            v = emit_mix(xx_xk, xx_xk[:, cc * 1024:(cc + 1) * 1024], cc,
                         sb_maa["k"][:, cc:cc + 1], xn, inplace_xx=True)
        mix_done = v

        def kevac_main(ot, ps, mmv):
            P.wait("dve", "s_pe", mmv)
            v = P.op("dve", lambda e, ot=ot: e.tensor_copy(
                k_fm[:, ot * 1024:(ot + 1) * 1024], ps[:, 0:1024]), sig="s_dve")
            return ("dve", v)

        emit_proj(wT["k"], lambda cc, hh: xx_xk[:, cc * 1024 + hh * 512:
                                                cc * 1024 + (hh + 1) * 512],
                  kevac_main, ring, wait_rhs=("s_dve", mix_done))
        emit_proj(wT["v"], lambda cc, hh: xv[:, cc * 1024 + hh * 512:
                                             cc * 1024 + (hh + 1) * 512],
                  emit_vproj_evac(vfm_tmp, v_tok), ring)
        P.tap("k_fm", k_fm[:])
        P.tap("v_tok", v_tok[:])
        emit_ascan(k_fm, v_tok, NCH, hist, kw)
        P.tap("hist", hist[:])
        P.barrier()

        ckpt(PH_MIX2)
        for cc in range(CCn):
            emit_mix(xr, xx_scr, cc, sb_maa["r"][:, cc:cc + 1], xn)
            v = emit_mix(xn, xx_scr, cc, sb_maa["g"][:, cc:cc + 1], xn,
                         inplace_xx=True)
        mix2_done = v

        def revac(ot, ps, mmv):
            P.wait("dve", "s_pe", mmv)
            v = P.op("dve", lambda e, ot=ot, ps=ps: e.tensor_copy(
                r_fm[:, ot * 1024:(ot + 1) * 1024], ps[:, 0:1024]), sig="s_dve")
            return ("dve", v)

        emit_proj(wT["r"], lambda cc, hh: xr[:, cc * 1024 + hh * 512:
                                             cc * 1024 + (hh + 1) * 512],
                  revac, ring, wait_rhs=("s_dve", mix2_done))

        def gevac(ot, ps, mmv):
            P.wait("act", "s_pe", mmv)
            v = P.op("act", lambda e, ot=ot, ps=ps: e.activation(
                g_fm[:, ot * 1024:(ot + 1) * 1024], ps[:, 0:1024], AF.Silu),
                sig="s_act")
            return ("act", v)

        emit_proj(wT["g"], lambda cc, hh: xn[:, cc * 1024 + hh * 512:
                                             cc * 1024 + (hh + 1) * 512],
                  gevac, ring)
        P.tap("r_fm", r_fm[:])
        P.tap("g_fm", g_fm[:])
        P.barrier()

        ckpt(PH_Y)
        for dst, key in [(sb_dpos, "DPOS"), (sb_mlow, "MLOW"), (sb_diag, "DIAG"),
                         (sb_qmodb, "QMODB")]:
            P.dma("sp", dst[:], conh[key].ap(), "s_in")
        for e in ("dve", "act", "pe"):
            P.wait(e, "s_in", P.val("s_in"))

        NGRP = H // 4
        unit = 0
        yn_done, gn_done, tr_done, z_done = {}, {}, {}, {}
        am_slot_guard = [0, 0]
        for grp in range(NGRP):
            heads = [grp * 4 + j for j in range(4)]
            ccs = [grp * 2, grp * 2 + 1]
            P.wait("dve", "s_act", P.val("s_act"))
            for j, h in enumerate(heads):
                v = P.op("dve", lambda e, h=h: e.tensor_single_scalar(
                    wmtmp[:], sb_dpos[:], sb_lnwhb[:, h:h + 1], ALU.mult),
                    sig="s_dve")
                P.wait("act", "s_dve", v)
                v = P.op("act", lambda e: e.activation(wmtmp[:], wmtmp[:], AF.Exp),
                         sig="s_act")
                P.wait("dve", "s_act", v)
                v = P.op("dve", lambda e: e.tensor_mul(wmtmp[:], wmtmp[:],
                                                       sb_mlow[:]), sig="s_dve")
                v = P.op("dve", lambda e, h=h, j=j: e.scalar_tensor_tensor(
                    wmat[:, j * 512:(j + 1) * 512], sb_diag[:],
                    sb_uhb[:, h:h + 1], wmtmp[:], ALU.mult, ALU.add), sig="s_dve")
            for jc, cc in enumerate(ccs):
                v = P.op("dve", lambda e, cc=cc: e.tensor_single_scalar(
                    wbtmp[:], sb_qmodb[:], sb_lnwpp[:, cc:cc + 1], ALU.mult),
                    sig="s_dve")
                P.wait("act", "s_dve", v)
                v = P.op("act", lambda e: e.activation(wbtmp[:], wbtmp[:], AF.Exp),
                         sig="s_act")
                P.wait("dve", "s_act", v)
                v = P.op("dve", lambda e, jc=jc, cc=cc: e.tensor_mul(
                    rwb[:, jc * 1024:(jc + 1) * 1024],
                    r_fm[:, cc * 1024:(cc + 1) * 1024], wbtmp[:]), sig="s_dve")
            wm_ready = P.val("s_dve")
            P.wait("pe", "s_dve", wm_ready)

            for c in range(NCH):
                am_done = {}
                if Y_STAGE < 2:
                    continue
                for j, h in enumerate(heads):
                    cc = ccs[j // 2]
                    rb = (h % 2) * 64
                    sl = (j % 2) * 512
                    ccol = cc * 1024 + c * 256
                    if am_slot_guard[j % 2] > 0:
                        P.wait("pe", "s_dve", am_slot_guard[j % 2])
                    for half in range(2):
                        v = P.op("pe", MM(
                            ps_big0[:, sl + half * 256:sl + (half + 1) * 256],
                            k_fm[rb:rb + 64, ccol + half * 128:ccol + half * 128 + 128],
                            r_fm[rb:rb + 64, ccol:ccol + 256],
                            True, True), sig="s_pe")
                    P.wait("dve", "s_pe", v)
                    am_done[j] = P.op("dve", lambda e, j=j, sl=sl: e.tensor_mul(
                        am[:, j * 512:(j + 1) * 512], ps_big0[:, sl:sl + 512],
                        wmat[:, j * 512:(j + 1) * 512]), sig="s_dve")
                    am_slot_guard[j % 2] = am_done[j]
                if Y_STAGE < 3:
                    continue
                for qs in range(2):
                    ysl = (unit % 2) * 512
                    if unit >= 2 and (unit - 2) in gn_done:
                        P.wait("pe", "s_dve", gn_done[unit - 2])
                    for j, h in enumerate(heads):
                        cc = ccs[j // 2]
                        rb = (h % 2) * 64
                        P.wait("pe", "s_dve", am_done[j])
                        v = P.op("pe", MM(
                            ps_big1[:, ysl + j * 64:ysl + (j + 1) * 64],
                            am[:, j * 512 + 0 * 256 + qs * 128:
                               j * 512 + 0 * 256 + qs * 128 + 128],
                            v_tok[:, (c * 2 + 0) * 2048 + h * 64:
                                  (c * 2 + 0) * 2048 + (h + 1) * 64],
                            True, False), sig="s_pe")
                        v = P.op("pe", MM(
                            ps_big1[:, ysl + j * 64:ysl + (j + 1) * 64],
                            am[:, j * 512 + 1 * 256 + qs * 128:
                               j * 512 + 1 * 256 + qs * 128 + 128],
                            v_tok[:, (c * 2 + 1) * 2048 + h * 64:
                                  (c * 2 + 1) * 2048 + (h + 1) * 64],
                            False, False), sig="s_pe")
                        v = P.op("pe", MM(
                            ps_big1[:, ysl + j * 64:ysl + (j + 1) * 64],
                            rwb[rb:rb + 64, (j // 2) * 1024 + c * 256 + qs * 128:
                                (j // 2) * 1024 + c * 256 + qs * 128 + 128],
                            hist[rb:rb + 64, c * 1024 + (h // 2) * 64:
                                 c * 1024 + (h // 2) * 64 + 64],
                            False, True), sig="s_pe")
                    ymm = v
                    if Y_STAGE < 4:
                        unit += 1
                        continue
                    up = (unit % 2)
                    so = up * 20
                    yo = up * 512
                    if unit >= 2 and (unit - 2) in yn_done:
                        P.wait("dve", "s_act", yn_done[unit - 2])
                    P.wait("dve", "s_pe", ymm)
                    ycp = P.op("dve", lambda e, ysl=ysl, yo=yo: e.tensor_copy(
                        ytile[:, yo:yo + 256], ps_big1[:, ysl:ysl + 256]),
                        sig="s_dve")
                    gn_done[unit] = ycp
                    P.selfwait("dve")
                    P.wait("act", "s_dve", ycp)
                    sq_v = P.op("act", lambda e, yo=yo: e.activation(
                        ytile[:, yo + 256:yo + 512], ytile[:, yo:yo + 256],
                        AF.Square), sig="s_act")
                    for j in range(4):
                        P.op("dve", lambda e, j=j, yo=yo, so=so: e.reduce_sum(
                            stat8[:, so + j:so + j + 1],
                            ytile[:, yo + j * 64:yo + (j + 1) * 64],
                            mybir.AxisListType.X), sig="s_dve")
                    P.wait("dve", "s_act", sq_v)
                    for j in range(4):
                        P.op("dve", lambda e, j=j, yo=yo, so=so: e.reduce_sum(
                            stat8[:, so + 4 + j:so + 5 + j],
                            ytile[:, yo + 256 + j * 64:yo + 256 + (j + 1) * 64],
                            mybir.AxisListType.X), sig="s_dve")
                    if Y_STAGE < 5:
                        yn_done[unit] = P.val("s_act")
                        unit += 1
                        continue
                    P.selfwait("dve")
                    P.op("dve", lambda e, so=so: e.tensor_single_scalar(
                        stat8[:, so + 8:so + 12], stat8[:, so:so + 4],
                        1.0 / HD, ALU.mult), sig="s_dve")
                    P.selfwait("dve")
                    P.op("dve", lambda e, so=so: e.tensor_mul(
                        stat8[:, so + 16:so + 20], stat8[:, so + 8:so + 12],
                        stat8[:, so + 8:so + 12]), sig="s_dve")
                    P.selfwait("dve")
                    v = P.op("dve", lambda e, so=so: e.scalar_tensor_tensor(
                        stat8[:, so + 12:so + 16], stat8[:, so + 4:so + 8],
                        1.0 / HD, stat8[:, so + 16:so + 20], ALU.mult,
                        ALU.subtract), sig="s_dve")
                    if Y_STAGE < 4.9:
                        yn_done[unit] = P.val("s_act")
                        unit += 1
                        continue
                    P.wait("act", "s_dve", v)
                    v = P.op("act", lambda e, so=so: e.activation(
                        stat8[:, so + 16:so + 20], stat8[:, so + 12:so + 16],
                        AF.Ln, bias=float(GN_EPS)), sig="s_act")
                    P.selfwait("act")
                    v = P.op("act", lambda e, so=so: e.activation(
                        stat8[:, so + 12:so + 16], stat8[:, so + 16:so + 20],
                        AF.Exp, scale=-0.5), sig="s_act")
                    P.wait("dve", "s_act", v)
                    gnv = P.op("dve", lambda e, so=so: e.scalar_tensor_tensor(
                        stat8[:, so + 16:so + 20], stat8[:, so + 8:so + 12],
                        -1.0, stat8[:, so + 12:so + 16], ALU.mult, ALU.mult),
                        sig="s_dve")
                    P.wait("act", "s_dve", gnv)
                    if Y_STAGE < 6:
                        yn_done[unit] = P.val("s_act")
                        unit += 1
                        continue
                    tpp = unit % 2
                    if unit >= 2 and (unit - 2) in tr_done:
                        P.wait("act", "s_pe", tr_done[unit - 2])
                    for j in range(4):
                        v = P.op("act", lambda e, j=j, yo=yo, so=so, tpp=tpp:
                                 e.activation(
                            ynst[:, tpp * 256 + j * 64:tpp * 256 + (j + 1) * 64],
                            ytile[:, yo + j * 64:yo + (j + 1) * 64],
                            AF.Identity, scale=stat8[:, so + 12 + j:so + 13 + j],
                            bias=stat8[:, so + 16 + j:so + 17 + j]), sig="s_act")
                    yn_done[unit] = v
                    if Y_STAGE < 7:
                        unit += 1
                        continue
                    P.wait("pe", "s_act", v)
                    if unit >= 1 and (unit - 1) in z_done:
                        P.wait("pe", "s_dve", z_done[unit - 1])
                    for t2 in range(2):
                        v = P.op("pe", lambda e, t2=t2, tpp=tpp: e.transpose(
                            ps_tr[:, tpp * 256 + t2 * 128:tpp * 256 + t2 * 128 + 128],
                            ynst[:, tpp * 256 + t2 * 128:tpp * 256 + t2 * 128 + 128],
                            ident[:]), sig="s_pe")
                    tr_done[unit] = v
                    P.wait("dve", "s_pe", v)
                    for t2 in range(2):
                        cc = ccs[t2]
                        zcol = cc * 1024 + c * 256 + qs * 128
                        P.op("dve", lambda e, t2=t2, cc=cc, tpp=tpp: e.tensor_scalar(
                            wbtmp[:, 256 + t2 * 128:256 + (t2 + 1) * 128],
                            ps_tr[:, tpp * 256 + t2 * 128:tpp * 256 + t2 * 128 + 128],
                            sb_gamma[:, cc:cc + 1], sb_beta[:, cc:cc + 1],
                            ALU.mult, ALU.add), sig="s_dve")
                        P.selfwait("dve")
                        v = P.op("dve", lambda e, t2=t2, zcol=zcol, cc=cc:
                                 e.tensor_mul(
                            z_fm[:, zcol:zcol + 128],
                            wbtmp[:, 256 + t2 * 128:256 + (t2 + 1) * 128],
                            g_fm[:, zcol:zcol + 128]), sig="s_dve")
                    z_done[unit] = v
                    unit += 1
        P.tap("z_fm", z_fm[:])
        P.tap("wmat", wmat[:])
        P.tap("rwb", rwb[:])
        P.tap("am", am[:])
        P.tap("ytile", ytile[:])
        P.tap("stat8", stat8[:])
        P.tap("ynst", ynst[:])
        P.barrier()

        ckpt(PH_WO)
        vx = P.dma("sp", x2[:].rearrange("p (cc t) -> p cc t", t=1024), xTv, "s_in")
        P.wait("dve", "s_in", vx)
        wsrc_o = wT["o"].rearrange("(cc p) o -> p cc o", p=128)
        mm_last_o = {}
        ev_o = {}
        for ot in range(CCn):
            slot = ot % 2
            ps = ps_big0 if ot % 2 == 0 else ps_big1
            if ot >= 2:
                P.wait("gp", "s_pe", mm_last_o[ot - 2])
            vw = P.dma("gp", ring2[:].rearrange("p (s cc o) -> p s cc o", s=2, o=128)
                       [:, slot], wsrc_o[:, :, ot * 128:(ot + 1) * 128], "s_w")
            P.wait("pe", "s_w", vw)
            if ot >= 2:
                P.wait("pe", "s_dve", ev_o[ot - 2])
            for cc in range(CCn):
                for hh in range(2):
                    v = P.op("pe", MM(
                        ps[:, hh * 512:(hh + 1) * 512],
                        ring2[:, slot * 2048 + cc * 128:slot * 2048 + (cc + 1) * 128],
                        z_fm[:, cc * 1024 + hh * 512:cc * 1024 + (hh + 1) * 512],
                        cc == 0, cc == CCn - 1), sig="s_pe")
            mm_last_o[ot] = v
            P.wait("dve", "s_pe", v)
            ev_o[ot] = P.op("dve", lambda e, ot=ot, ps=ps: e.tensor_add(
                x2[:, ot * 1024:(ot + 1) * 1024],
                x2[:, ot * 1024:(ot + 1) * 1024], ps[:, 0:1024]), sig="s_dve")
        P.tap("x2", x2[:])
        P.barrier()

        ckpt(PH_LN2)
        for cc in range(CCn):
            v = P.op("dve", lambda e, cc=cc: e.tensor_copy(
                x2_bf[:, cc * 1024:(cc + 1) * 1024],
                x2[:, cc * 1024:(cc + 1) * 1024]), sig="s_dve")
        for e in ("pe", "act"):
            P.wait(e, "s_dve", v)
        emit_ln(x2_bf, mu2_b, rstd2_b, rows2, rowscr2, sqscr2, LN_EPS)
        for cc in range(CCn):
            P.op("dve", lambda e, cc=cc: e.tensor_sub(
                h_fm[:, cc * 1024:(cc + 1) * 1024],
                x2[:, cc * 1024:(cc + 1) * 1024], mu2_b[:]), sig="s_dve")
            v = P.op("dve", lambda e, cc=cc: e.tensor_mul(
                h_fm[:, cc * 1024:(cc + 1) * 1024],
                h_fm[:, cc * 1024:(cc + 1) * 1024], rstd2_b[:]), sig="s_dve")
        P.tap("h_fm", h_fm[:])
        P.barrier()

        ckpt(PH_FFN)
        for cc in range(CCn):
            for hh in range(2):
                v = P.op("pe", MM(ps_big0[0:1, hh * 512:(hh + 1) * 512],
                                  ones_c[:], h_fm[:, cc * 1024 + hh * 512:
                                                  cc * 1024 + (hh + 1) * 512],
                                  cc == 0, cc == CCn - 1), sig="s_pe")
        P.wait("dve", "s_pe", v)
        v = P.op("dve", lambda e: e.tensor_single_scalar(
            cneg_bf[0:1, 0:1024], ps_big0[0:1, 0:1024], -128.0, ALU.mult),
            sig="s_dve")
        P.wait("pe", "s_dve", v)

        gsrc = gateT.rearrange("(cc p) i -> p cc i", p=128)
        usrc = upT.rearrange("(cc p) i -> p cc i", p=128)
        gu_last = {}
        sg_done = {}
        f_done = {}
        dn_last = {}
        dn_evac = {}
        prev_pe0 = P.val("s_pe")
        for qq in range(4):
            for itl in range(16):
                it = qq * 16 + itl
                gslot = (2 * it) % 4
                uslot = (2 * it + 1) % 4
                P.wait("gp", "s_pe", gu_last[it - 2] if it >= 2 else prev_pe0)
                vg = P.dma("gp", ring_gu[:].rearrange(
                    "p (s cc o) -> p s cc o", s=4, o=128)[:, gslot],
                    gsrc[:, :, it * 128:(it + 1) * 128], "s_w")
                vu = P.dma("gp", ring_gu[:].rearrange(
                    "p (s cc o) -> p s cc o", s=4, o=128)[:, uslot],
                    usrc[:, :, it * 128:(it + 1) * 128], "s_w")
                if it >= 1:
                    P.wait("pe", "s_act", sg_done[it - 1])
                    P.wait("pe", "s_dve", f_done[it - 1])
                P.wait("pe", "s_w", vg)
                for hh in range(2):
                    P.op("pe", MM(ps_big0[:, hh * 512:(hh + 1) * 512], ones_rb[:],
                                  cneg_bf[0:1, hh * 512:(hh + 1) * 512],
                                  True, False), sig="s_pe")
                    for cc in range(CCn):
                        P.op("pe", MM(
                            ps_big0[:, hh * 512:(hh + 1) * 512],
                            ring_gu[:, gslot * 2048 + cc * 128:
                                    gslot * 2048 + (cc + 1) * 128],
                            h_fm[:, cc * 1024 + hh * 512:cc * 1024 + (hh + 1) * 512],
                            False, cc == CCn - 1), sig="s_pe")
                P.wait("pe", "s_w", vu)
                for hh in range(2):
                    P.op("pe", MM(ps_big1[:, hh * 512:(hh + 1) * 512], ones_rb[:],
                                  cneg_bf[0:1, hh * 512:(hh + 1) * 512],
                                  True, False), sig="s_pe")
                    for cc in range(CCn):
                        v = P.op("pe", MM(
                            ps_big1[:, hh * 512:(hh + 1) * 512],
                            ring_gu[:, uslot * 2048 + cc * 128:
                                    uslot * 2048 + (cc + 1) * 128],
                            h_fm[:, cc * 1024 + hh * 512:cc * 1024 + (hh + 1) * 512],
                            False, cc == CCn - 1), sig="s_pe")
                gu_last[it] = v
                P.wait("act", "s_pe", v)
                sg_done[it] = P.op("act", lambda e, it=it: e.activation(
                    sg_tmp[:], ps_big0[:, 0:1024], AF.Silu,
                    scale=sb_gs[:, it:it + 1]), sig="s_act")
                P.wait("dve", "s_act", sg_done[it])
                P.op("dve", lambda e: e.tensor_mul(
                    f0_tmp[:], sg_tmp[:], ps_big1[:, 0:1024]), sig="s_dve")
                f_done[it] = P.op("dve", lambda e, it=it, itl=itl:
                                  e.tensor_single_scalar(
                    f_sb[:, itl * 1024:(itl + 1) * 1024], f0_tmp[:],
                    sb_us[:, it:it + 1], ALU.mult), sig="s_dve")
            P.wait("pe", "s_dve", f_done[qq * 16 + 15])
            for ic in range(CCn):
                for hh in range(2):
                    v = P.op("pe", MM(ps_big0[0:1, hh * 512:(hh + 1) * 512],
                                      ones_c[:],
                                      f_sb[:, ic * 1024 + hh * 512:
                                           ic * 1024 + (hh + 1) * 512],
                                      ic == 0, ic == CCn - 1), sig="s_pe")
            P.wait("dve", "s_pe", v)
            v = P.op("dve", lambda e: e.tensor_single_scalar(
                cb_rows[0:1, 0:1024], ps_big0[0:1, 0:1024], -128.0, ALU.mult),
                sig="s_dve")
            P.selfwait("dve")
            v = P.op("dve", lambda e: e.tensor_copy(
                cneg_bf[0:1, 1024:2048], cb_rows[0:1, 0:1024]), sig="s_dve")
            P.selfwait("dve")
            v = P.op("dve", lambda e: e.tensor_sub(
                cb_rows[0:1, 1024:2048], cb_rows[0:1, 0:1024],
                cneg_bf[0:1, 1024:2048]), sig="s_dve")
            P.selfwait("dve")
            v = P.op("dve", lambda e: e.tensor_copy(
                cneg_bf[0:1, 2048:3072], cb_rows[0:1, 1024:2048]), sig="s_dve")
            P.wait("pe", "s_dve", v)
            dsrc = downT[qq * 2048:(qq + 1) * 2048, :].rearrange(
                "(ic p) hcol -> p ic hcol", p=128)
            prev_pe1 = P.val("s_pe")
            for ht in range(CCn):
                dslot = ht % 4
                P.wait("gp", "s_pe", dn_last[(qq, ht - 4)] if ht >= 4 else prev_pe1)
                vw = P.dma("gp", ring_dn[:].rearrange(
                    "p (s ic hcol) -> p s ic hcol", s=4, hcol=128)[:, dslot],
                    dsrc[:, :, ht * 128:(ht + 1) * 128], "s_w")
                P.wait("pe", "s_w", vw)
                if ht >= 1:
                    P.wait("pe", "s_dve", dn_evac[(qq, ht - 1)])
                for hh in range(2):
                    P.op("pe", MM(ps_state[:, hh * 512:(hh + 1) * 512], ones_rb[:],
                                  cneg_bf[0:1, 1024 + hh * 512:1024 + (hh + 1) * 512],
                                  True, False), sig="s_pe")
                    P.op("pe", MM(ps_state[:, hh * 512:(hh + 1) * 512], ones_rb[:],
                                  cneg_bf[0:1, 2048 + hh * 512:2048 + (hh + 1) * 512],
                                  False, False), sig="s_pe")
                    for ic in range(CCn):
                        v = P.op("pe", MM(
                            ps_state[:, hh * 512:(hh + 1) * 512],
                            ring_dn[:, dslot * 2048 + ic * 128:
                                    dslot * 2048 + (ic + 1) * 128],
                            f_sb[:, ic * 1024 + hh * 512:ic * 1024 + (hh + 1) * 512],
                            False, ic == CCn - 1), sig="s_pe")
                dn_last[(qq, ht)] = v
                P.wait("dve", "s_pe", v)
                dn_evac[(qq, ht)] = P.op("dve", lambda e, ht=ht:
                                         e.scalar_tensor_tensor(
                    x2[:, ht * 1024:(ht + 1) * 1024], ps_state[:, 0:1024],
                    sb_ds[:, ht:ht + 1], x2[:, ht * 1024:(ht + 1) * 1024],
                    ALU.mult, ALU.add), sig="s_dve")
        P.barrier()

        vo = P.dma("sp", out_hbm.rearrange("(cc p) t -> p cc t", p=128),
                   x2[:].rearrange("p (cc t) -> p cc t", t=1024), "s_out")
        P.wait("sp", "s_out", P.val("s_out"))


    except _PhaseStop:
        P.barrier()
        v = P.op("dve", lambda e: e.memset(x2[:], 0.0), sig="s_dve")
        P.wait("sp", "s_dve", v)
        vo = P.dma("sp", out_hbm.rearrange("(cc p) t -> p cc t", p=128),
                   x2[:].rearrange("p (cc t) -> p cc t", t=1024), "s_out")
        P.wait("sp", "s_out", P.val("s_out"))

    with nc.Block() as block:
        @block.tensor
        def _(e):
            for fn in P.stream["pe"]:
                fn(e)

        @block.vector
        def _(e):
            for fn in P.stream["dve"]:
                fn(e)

        @block.scalar
        def _(e):
            for fn in P.stream["act"]:
                fn(e)

        @block.gpsimd
        def _(e):
            for fn in P.stream["gp"]:
                fn(e)

        @block.sync
        def _(e):
            for fn in P.stream["sp"]:
                fn(e)

    return nc, P


_PROG_CACHE = {}


def _get_prog():
    key = tuple(DEBUG_TAPS)
    if key not in _PROG_CACHE:
        _PROG_CACHE[key] = build_program()
    return _PROG_CACHE[key]


def _pp(vec, ncols):
    return np.ascontiguousarray(
        np.asarray(vec, np.float32).reshape(-1).reshape(ncols, 128).T)


def kernel(L0_attn_x, L0_attn_time_maa_k, L0_attn_time_maa_v,
           L0_attn_time_maa_r, L0_attn_time_maa_g, L0_attn_time_decay,
           L0_attn_time_faaaa, L0_attn_Wr, L0_attn_Wk, L0_attn_Wv,
           L0_attn_Wg, L0_attn_Wo, L0_attn_gamma, L0_attn_beta,
           L0_ffn_gate_w, L0_ffn_gate_scale, L0_ffn_up_w, L0_ffn_up_scale,
           L0_ffn_down_w, L0_ffn_down_scale):
    nc, P = _get_prog()
    x = np.asarray(L0_attn_x, np.float32)

    shared = {
        "wrT": np.ascontiguousarray(np.asarray(L0_attn_Wr, np.float32).T),
        "wkT": np.ascontiguousarray(np.asarray(L0_attn_Wk, np.float32).T),
        "wvT": np.ascontiguousarray(np.asarray(L0_attn_Wv, np.float32).T),
        "wgT": np.ascontiguousarray(np.asarray(L0_attn_Wg, np.float32).T),
        "woT": np.ascontiguousarray(np.asarray(L0_attn_Wo, np.float32).T),
        "gateT": np.ascontiguousarray(np.asarray(L0_ffn_gate_w, np.int32).T),
        "upT": np.ascontiguousarray(np.asarray(L0_ffn_up_w, np.int32).T),
        "downT": np.ascontiguousarray(np.asarray(L0_ffn_down_w, np.int32).T),
        "maa_k": _pp(L0_attn_time_maa_k, 16),
        "maa_v": _pp(L0_attn_time_maa_v, 16),
        "maa_r": _pp(L0_attn_time_maa_r, 16),
        "maa_g": _pp(L0_attn_time_maa_g, 16),
        "gamma_pp": _pp(L0_attn_gamma, 16),
        "beta_pp": _pp(L0_attn_beta, 16),
        "gs_pp": _pp(L0_ffn_gate_scale, 64),
        "us_pp": _pp(L0_ffn_up_scale, 64),
        "ds_pp": _pp(L0_ffn_down_scale, 16),
        "decay": np.ascontiguousarray(
            np.asarray(L0_attn_time_decay, np.float32).reshape(32, 1)),
        "faaaa": np.ascontiguousarray(
            np.asarray(L0_attn_time_faaaa, np.float32).reshape(32, 1)),
    }

    in_maps = []
    for core in range(8):
        b, qq = core // 4, core % 4
        t0 = qq * NT
        m = dict(shared)
        m["xT"] = np.ascontiguousarray(x[b, t0:t0 + NT].T)
        win = np.zeros((PREF, C), np.float32)
        if t0 > 0:
            win[PREF - t0:] = x[b, 0:t0]
        m["xP"] = np.ascontiguousarray(win.T)
        in_maps.append(m)

    res = run_bass_kernel_spmd(nc, in_maps, core_ids=list(range(8)))

    full = np.empty((B, T, C), np.float32)
    for core in range(8):
        b, qq = core // 4, core % 4
        full[b, qq * NT:(qq + 1) * NT] = res.results[core]["out_fm"].T
    if DEBUG_TAPS:
        kernel.last_results = res
    return full


# version 08
# speedup vs baseline: 1.0017x; 1.0017x over previous
import numpy as np
import ml_dtypes
import concourse.bass as bass
import concourse.mybir as mybir
from concourse.bass_utils import run_bass_kernel_spmd

F32 = mybir.dt.float32
BF16 = mybir.dt.bfloat16
I32 = mybir.dt.int32
AF = mybir.ActivationFunctionType
ALU = mybir.AluOpType
BF = ml_dtypes.bfloat16

B, T, C = 2, 4096, 2048
H, HD = 32, 64
NT = 1024
Q = 256
NCH = NT // Q
PREF = 3072
SBLK = 1024
NSB = PREF // SBLK
CCn = 16
FI = 8192
LN_EPS = 1e-5
GN_EPS = 1e-5 * 144
SBUF_BASE = 16512
USABLE = 212832

DEBUG_TAPS = []
PHASE_LIMIT = None
Y_STAGE = 7


class _PhaseStop(Exception):
    pass


def _consts():
    c = {}
    c["ident"] = np.eye(128, dtype=BF)
    c["ident32"] = np.eye(32, dtype=np.float32)
    c["ones_c"] = np.ones((128, 1), BF)
    c["ones_rb"] = np.ones((1, 128), BF)
    c["ones_rf"] = np.ones((1, 128), np.float32)
    hp = np.zeros((32, 128), np.float32)
    for h in range(32):
        hp[h, (h % 2) * 64:(h % 2) * 64 + 64] = 1.0
    c["HP"] = hp
    hc = np.zeros((32, 16), np.float32)
    for h in range(32):
        hc[h, h // 2] = 1.0
    c["HC"] = hc
    c["QMODB"] = np.broadcast_to((np.arange(1024) % Q).astype(np.float32),
                                 (128, 1024)).copy()
    c["REVQ"] = np.stack([255.0 - np.arange(128), 127.0 - np.arange(128)],
                         axis=1).astype(np.float32)
    dpos, mlow, diag = [], [], []
    for half in range(2):
        qp = half * 128 + np.arange(128)[:, None]
        qq = np.arange(256)[None, :]
        dpos.append(np.maximum(qq - qp - 1, 0).astype(np.float32))
        mlow.append((qq > qp).astype(np.float32))
        diag.append((qq == qp).astype(np.float32))
    c["DPOS"] = np.concatenate(dpos, axis=1)
    c["MLOW"] = np.concatenate(mlow, axis=1)
    c["DIAG"] = np.concatenate(diag, axis=1)
    c["E32"] = np.repeat(np.eye(32, dtype=np.float32), 64, axis=1)
    cv = np.array([0.0, 1.0, -1.0, -0.5, 256.0, LN_EPS, GN_EPS, -128.0],
                  np.float32)
    c["CVALS"] = np.broadcast_to(cv, (128, 8)).copy()
    return c


class Prog:
    ENGS = ("pe", "dve", "act", "gp", "sp")

    def __init__(self, nc):
        self.nc = nc
        self.stream = {e: [] for e in self.ENGS}
        self.sems = {}
        self.cnt = {}
        self._alloc = []
        self.tensors = {}
        self.tap_names = []

    def sem(self, name):
        h = self.nc.semaphore(name).__enter__()
        self.sems[name] = h
        self.cnt[name] = 0

    def inc(self, name, n=1):
        self.cnt[name] += n
        return self.cnt[name]

    def val(self, name):
        return self.cnt[name]

    def emit(self, eng, fn):
        self.stream[eng].append(fn)

    def wait(self, eng, sname, v):
        if v <= 0:
            return
        h = self.sems[sname]
        self.emit(eng, lambda e, h=h, v=v: e.wait_ge(h, v))

    def op(self, eng, fn, sig=None, inc=1):
        if sig is None:
            self.emit(eng, fn)
            return None
        v = self.inc(sig, inc)
        h = self.sems[sig]
        self.emit(eng, lambda e, fn=fn, h=h, n=inc: fn(e).then_inc(h, n))
        return v

    def selfwait(self, eng):
        sem = {"dve": "s_dve", "act": "s_act", "pe": "s_pe"}[eng]
        self.wait(eng, sem, self.val(sem))

    def dma(self, eng, dst, src, sig):
        return self.op(eng, lambda e, d=dst, s=src: e.dma_start(d, s),
                       sig=sig, inc=16)

    def barrier(self):
        own = {"pe": ["s_pe"], "dve": ["s_dve"], "act": ["s_act"],
               "gp": ["s_w"], "sp": ["s_in", "s_out"]}
        for e in self.ENGS:
            for s in own[e]:
                self.wait(e, s, self.val(s))
        tgt = {}
        for e in self.ENGS:
            tgt[e] = self.inc("b_" + e, 1)
            h = self.sems["b_" + e]
            self.emit(e, lambda eng, h=h: eng.nop().then_inc(h, 1))
        for e in self.ENGS:
            for o in self.ENGS:
                if o != e:
                    self.wait(e, "b_" + o, tgt[o])

    def buf(self, name, shape, dtype, s_phase, e_phase, align=64, at=None):
        size = int(np.prod(shape[1:])) * mybir.dt.size(dtype)
        size = (size + align - 1) // align * align
        if at is not None:
            off = at
        else:
            spans = sorted((o, en) for (o, en, s, e) in self._alloc
                           if not (e < s_phase or s > e_phase))
            off = 0
            for (o, en) in spans:
                if off + size <= o:
                    break
                off = max(off, en)
            assert off + size <= USABLE, ("sbuf overflow", name, off, size)
            self._alloc.append((off, off + size, s_phase, e_phase))
        t = self.nc.alloc_sbuf_tensor_at(name, shape, dtype,
                                         offset=SBUF_BASE + off)
        self.offs = getattr(self, "offs", {})
        self.offs[name] = off
        self.tensors[name] = t
        return t

    def tap(self, name, ap):
        if name not in DEBUG_TAPS:
            return
        out = self.nc.dram_tensor("tap_" + name, list(ap.shape), ap.dtype,
                                  kind="ExternalOutput").ap()
        for s in ("s_pe", "s_dve", "s_act"):
            self.wait("sp", s, self.val(s))
        v = self.dma("sp", out, ap, "s_out")
        self.wait("sp", "s_out", v)
        self.tap_names.append(name)


def build_program():
    nc = bass.Bass("TRN2", target_bir_lowering=False)
    P = Prog(nc)
    CON = _consts()

    def din(name, shape, dtype):
        return nc.dram_tensor(name, shape, dtype, kind="ExternalInput").ap()

    xT = din("xT", [C, NT], F32)
    xP = din("xP", [C, PREF], F32)
    wT = {n: din("w" + n + "T", [C, C], F32) for n in "rkvgo"}
    gateT = din("gateT", [C, FI], I32)
    upT = din("upT", [C, FI], I32)
    downT = din("downT", [FI, C], I32)
    maa = {m: din("maa_" + m, [128, 16], F32) for m in "kvrg"}
    gamma_pp = din("gamma_pp", [128, 16], F32)
    beta_pp = din("beta_pp", [128, 16], F32)
    gs_pp = din("gs_pp", [128, 64], F32)
    us_pp = din("us_pp", [128, 64], F32)
    ds_pp = din("ds_pp", [128, 16], F32)
    decay = din("decay", [32, 1], F32)
    faaaa = din("faaaa", [32, 1], F32)
    out_hbm = nc.dram_tensor("out_fm", [C, NT], F32, kind="ExternalOutput").ap()

    conh = {k: nc.inline_tensor(np.ascontiguousarray(v), name="c_" + k)
            for k, v in CON.items()}

    for s in ("s_pe", "s_dve", "s_act", "s_w", "s_in", "s_out"):
        P.sem(s)
    for e in P.ENGS:
        P.sem("b_" + e)

    (PH_PRO, PH_PRE, PH_LN1, PH_MIX1, PH_KP, PH_VP, PH_SC, PH_MIX2, PH_RP,
     PH_GP, PH_Y, PH_WO, PH_LN2, PH_FFN, PH_OUT) = range(15)

    ident = P.buf("ident", [128, 128], BF16, PH_PRO, PH_Y)
    ones_c = P.buf("ones_c", [128, 1], BF16, PH_PRO, PH_FFN)
    ones_rb = P.buf("ones_rb", [1, 128], BF16, PH_PRO, PH_FFN)
    ones_rf = P.buf("ones_rf", [1, 128], F32, PH_PRO, PH_FFN)
    sb_maa = {m: P.buf("maa" + m, [128, 16], F32, PH_PRO, PH_MIX2)
              for m in "kvrg"}
    sb_gamma = P.buf("gammapp", [128, 16], F32, PH_PRO, PH_Y)
    sb_beta = P.buf("betapp", [128, 16], F32, PH_PRO, PH_Y)
    sb_gs = P.buf("gspp", [128, 64], F32, PH_PRO, PH_FFN)
    sb_us = P.buf("uspp", [128, 64], F32, PH_PRO, PH_FFN)
    sb_ds = P.buf("dspp", [128, 16], F32, PH_PRO, PH_FFN)
    sb_decay = P.buf("decay", [32, 1], F32, PH_PRO, PH_PRO)
    sb_u32 = P.buf("u32", [32, 1], F32, PH_PRO, PH_PRO)
    sb_lnw = P.buf("lnw", [32, 1], F32, PH_PRO, PH_PRO)
    sb_id32 = P.buf("id32", [32, 32], F32, PH_PRO, PH_PRO)
    sb_hp = P.buf("HP", [32, 128], F32, PH_PRO, PH_PRO)
    sb_hc = P.buf("HC", [32, 16], F32, PH_PRO, PH_PRO)
    sb_revq = P.buf("REVQ", [128, 2], F32, PH_PRO, PH_PRO)
    sb_scr32 = P.buf("scr32", [64, 32], F32, PH_PRO, PH_PRO)
    sb_scr32b = P.buf("scr32b", [1, 32], F32, PH_PRO, PH_PRO)
    sb_e32 = P.buf("E32", [32, 2048], F32, PH_PRO, PH_PRO)
    sb_lnwpp = P.buf("lnwpp", [128, 16], F32, PH_PRO, PH_Y)
    sb_wspp = P.buf("wspp", [128, 16], F32, PH_PRO, PH_SC)
    sb_cvals = P.buf("cvals", [128, 8], F32, PH_PRO, PH_OUT)
    for _i, _v in enumerate([0.0, 1.0, -1.0, -0.5, 256.0, LN_EPS, GN_EPS,
                             -128.0]):
        nc.const_aps.register(F32, float(_v), sb_cvals[:, _i:_i + 1])
    sb_lnwhb = P.buf("lnwhb", [128, 32], F32, PH_PRO, PH_Y)
    sb_uhb = P.buf("uhb", [128, 32], F32, PH_PRO, PH_Y)
    sb_wk = P.buf("wksm", [128, 64], F32, PH_PRO, PH_SC)
    sb_xnprev = P.buf("xnprev", [128, 16], BF16, PH_PRO, PH_MIX2)
    state_sb = P.buf("state", [128, 1024], F32, PH_PRO, PH_SC)

    xn = P.buf("xn", [128, CCn * 1024], BF16, PH_LN1, PH_GP)
    k_fm = P.buf("k_fm", [128, CCn * 1024], BF16, PH_KP, PH_Y)
    v_tok = P.buf("v_tok", [128, 8 * 2048], BF16, PH_VP, PH_Y)
    hist = P.buf("hist", [128, NCH * 1024], BF16, PH_SC, PH_Y)
    r_fm = P.buf("r_fm", [128, CCn * 1024], BF16, PH_RP, PH_Y)
    g_fm = P.buf("g_fm", [128, CCn * 1024], BF16, PH_GP, PH_Y)
    z_fm = P.buf("z_fm", [128, CCn * 1024], BF16, PH_Y, PH_WO)
    x2 = P.buf("x2", [128, CCn * 1024], F32, PH_WO, PH_OUT)
    h_fm = P.buf("h_fm", [128, CCn * 1024], BF16, PH_LN2, PH_FFN)
    f_sb = P.buf("f", [128, CCn * 1024], BF16, PH_FFN, PH_FFN)
    xx_xk = P.buf("xx_xk", [128, CCn * 1024], BF16, PH_MIX1, PH_KP)
    xv = P.buf("xv", [128, CCn * 1024], BF16, PH_MIX1, PH_VP)
    xr = P.buf("xr", [128, CCn * 1024], BF16, PH_MIX2, PH_RP)
    xnP = P.buf("xnP", [128, CCn * 1024], BF16, PH_PRE, PH_PRE)
    xkP = P.buf("xkP", [128, CCn * 1024], BF16, PH_PRE, PH_PRE)
    xvP = P.buf("xvP", [128, CCn * 1024], BF16, PH_PRE, PH_PRE)
    k_fmP = P.buf("k_fmP", [128, CCn * 1024], BF16, PH_PRE, PH_PRE)
    v_tokP = P.buf("v_tokP", [128, 8 * 2048], BF16, PH_PRE, PH_PRE)
    ringP = P.buf("ringP", [128, 2 * 2048], BF16, PH_PRE, PH_PRE)
    vfmP = P.buf("vfmP", [128, 1024], BF16, PH_PRE, PH_PRE)
    kwP = P.buf("kwP", [128, 2 * 2048], BF16, PH_PRE, PH_PRE,
                at=P.offs["ringP"])
    x_bf = P.buf("x_bf", [128, CCn * 1024], BF16, PH_LN1, PH_LN1)
    xq = P.buf("xq", [128, CCn * 256], F32, PH_LN1, PH_LN1)
    mu_b = P.buf("mu_b", [128, 1024], F32, PH_PRE, PH_LN1)
    rstd_b = P.buf("rstd_b", [128, 1024], F32, PH_PRE, PH_LN1)
    rows1 = P.buf("rows1", [1, 2048], F32, PH_PRE, PH_LN1)
    rowscr = P.buf("rowscr", [1, 1024], F32, PH_PRE, PH_LN1)
    sqscr = P.buf("sqscr", [128, 2 * 1024], BF16, PH_PRE, PH_LN1)
    ring = P.buf("ring", [128, 2 * 2048], BF16, PH_KP, PH_GP)
    vfm_tmp = P.buf("vfm", [128, 1024], BF16, PH_VP, PH_VP)
    kw = P.buf("kw", [128, 2 * 2048], BF16, PH_SC, PH_SC)
    xx_scr = P.buf("xxscr", [128, 1024], BF16, PH_MIX2, PH_MIX2)
    wmat = P.buf("wmat", [128, 4 * 512], BF16, PH_Y, PH_Y)
    rwb = P.buf("rwb", [128, 2 * 1024], BF16, PH_Y, PH_Y)
    wbtmp = P.buf("wbtmp", [128, 1024], F32, PH_Y, PH_Y)
    ynst = P.buf("ynst", [128, 2 * 256], BF16, PH_Y, PH_Y)
    am = P.buf("am", [128, 4 * 512], BF16, PH_Y, PH_Y)
    stat8 = P.buf("stat8", [128, 2 * 20], F32, PH_Y, PH_Y)
    ytile = P.buf("ytile", [128, 2 * 512], F32, PH_Y, PH_Y)
    sb_dpos = P.buf("DPOS", [128, 512], F32, PH_Y, PH_Y)
    sb_mlow = P.buf("MLOW", [128, 512], F32, PH_Y, PH_Y)
    sb_diag = P.buf("DIAG", [128, 512], F32, PH_Y, PH_Y)
    sb_qmodb = P.buf("QMODB", [128, 1024], F32, PH_Y, PH_Y)
    wmtmp = P.buf("wmtmp", [128, 512], F32, PH_Y, PH_Y)
    ring2 = P.buf("ring2", [128, 2 * 2048], BF16, PH_WO, PH_WO)
    x2_bf = P.buf("x2bf", [128, CCn * 1024], BF16, PH_LN2, PH_LN2)
    mu2_b = P.buf("mu2b", [128, 1024], F32, PH_LN2, PH_LN2)
    rstd2_b = P.buf("rstd2b", [128, 1024], F32, PH_LN2, PH_LN2)
    rows2 = P.buf("rows2", [1, 2048], F32, PH_LN2, PH_FFN)
    rowscr2 = P.buf("rowscr2", [1, 1024], F32, PH_LN2, PH_FFN)
    sqscr2 = P.buf("sqscr2", [128, 2 * 1024], BF16, PH_LN2, PH_LN2)
    ring_gu = P.buf("ring_gu", [128, 4 * 2048], BF16, PH_FFN, PH_FFN)
    ring_dn = P.buf("ring_dn", [128, 4 * 2048], BF16, PH_FFN, PH_FFN)
    cb_rows = P.buf("cbrows", [1, 2048], F32, PH_LN2, PH_FFN)
    cneg_bf = P.buf("cnegbf", [1, 3 * 1024], BF16, PH_LN2, PH_FFN)
    sg_tmp = P.buf("sgtmp", [128, 1024], F32, PH_FFN, PH_FFN)
    f0_tmp = P.buf("f0tmp", [128, 1024], F32, PH_FFN, PH_FFN)

    ps_big0 = nc.alloc_psum_tensor("ps_big0", [128, 1024], F32)
    ps_big1 = nc.alloc_psum_tensor("ps_big1", [128, 1024], F32)
    ps_state = nc.alloc_psum_tensor("ps_state", [128, 1024], F32)
    ps_tr = nc.alloc_psum_tensor("ps_tr", [128, 1024], BF16)
    ps_sml = nc.alloc_psum_tensor("ps_sml", [128, 512], F32)

    MM = lambda o, l, r, st, sp, tp=None: (
        lambda e, o=o, l=l, r=r, st=st, sp=sp, tp=tp:
        e.matmul(o, l, r, start=st, stop=sp, tile_position=tp))

    def ckpt(ph):
        if PHASE_LIMIT is not None and ph > PHASE_LIMIT:
            raise _PhaseStop()

    try:
        for dst, src in [
            (ident[:], conh["ident"].ap()), (ones_c[:], conh["ones_c"].ap()),
            (ones_rb[:], conh["ones_rb"].ap()), (ones_rf[:], conh["ones_rf"].ap()),
            (sb_id32[:], conh["ident32"].ap()),
            (sb_hp[:], conh["HP"].ap()), (sb_hc[:], conh["HC"].ap()),
            (sb_revq[:], conh["REVQ"].ap()), (sb_e32[:], conh["E32"].ap()),
            (sb_maa["k"][:], maa["k"]), (sb_maa["v"][:], maa["v"]),
            (sb_maa["r"][:], maa["r"]), (sb_maa["g"][:], maa["g"]),
            (sb_gamma[:], gamma_pp), (sb_beta[:], beta_pp),
            (sb_gs[:], gs_pp), (sb_us[:], us_pp), (sb_ds[:], ds_pp),
            (sb_decay[:], decay), (sb_u32[:], faaaa),
            (sb_cvals[:], conh["CVALS"].ap()),
        ]:
            P.dma("sp", dst, src, "s_in")
        for e in ("act", "dve", "pe"):
            P.wait(e, "s_in", P.val("s_in"))

        v = P.op("act", lambda e: e.activation(sb_lnw[:], sb_decay[:], AF.Exp),
                 sig="s_act")
        P.wait("dve", "s_act", v)
        v = P.op("dve", lambda e: e.tensor_scalar_mul(sb_lnw[:], sb_lnw[:], -1.0),
                 sig="s_dve")
        P.wait("pe", "s_dve", v)

        v = P.op("pe", MM(ps_sml[0:1, 0:32], sb_lnw[:], sb_id32[:], True, True),
                 sig="s_pe")
        v = P.op("pe", MM(ps_sml[0:1, 32:64], sb_u32[:], sb_id32[:], True, True),
                 sig="s_pe")
        P.wait("dve", "s_pe", v)
        v = P.op("dve", lambda e: e.tensor_copy(sb_scr32[0:1, 0:32],
                                                ps_sml[0:1, 0:32]), sig="s_dve")
        v = P.op("dve", lambda e: e.tensor_copy(sb_scr32b[0:1, 0:32],
                                                ps_sml[0:1, 32:64]), sig="s_dve")
        P.wait("pe", "s_dve", v)
        v = P.op("pe", MM(ps_sml[:, 0:32], ones_rf[:], sb_scr32[0:1, 0:32],
                          True, True), sig="s_pe")
        v = P.op("pe", MM(ps_sml[:, 32:64], ones_rf[:], sb_scr32b[0:1, 0:32],
                          True, True), sig="s_pe")
        P.wait("dve", "s_pe", v)
        v = P.op("dve", lambda e: e.tensor_copy(sb_lnwhb[:], ps_sml[:, 0:32]),
                 sig="s_dve")
        v = P.op("dve", lambda e: e.tensor_copy(sb_uhb[:], ps_sml[:, 32:64]),
                 sig="s_dve")
        P.wait("act", "s_dve", v)
        P.wait("pe", "s_dve", v)
        for half in range(2):
            v = P.op("dve", lambda e, h=half: e.tensor_single_scalar(
                sb_wk[:, h * 32:h * 32 + 32], sb_lnwhb[:], sb_revq[:, h:h + 1],
                ALU.mult), sig="s_dve")
            P.wait("act", "s_dve", v)
            P.op("act", lambda e, h=half: e.activation(
                sb_wk[:, h * 32:h * 32 + 32], sb_wk[:, h * 32:h * 32 + 32],
                AF.Exp), sig="s_act")

        def make_pp(dst, src_col):
            v = P.op("dve", lambda e, s=src_col: e.tensor_single_scalar(
                sb_scr32[0:32, 16:32], sb_hc[:], s[:], ALU.mult), sig="s_dve")
            P.wait("pe", "s_dve", v)
            v = P.op("pe", MM(ps_sml[:, 64:80], sb_hp[:], sb_scr32[0:32, 16:32],
                              True, True), sig="s_pe")
            P.wait("dve", "s_pe", v)
            v = P.op("dve", lambda e, d=dst: e.tensor_copy(d[:], ps_sml[:, 64:80]),
                     sig="s_dve")
            P.wait("pe", "s_dve", v)
            return v

        make_pp(sb_lnwpp, sb_lnw)
        v = make_pp(sb_wspp, sb_lnw)
        P.wait("act", "s_dve", v)
        P.op("act", lambda e: e.activation(sb_wspp[:], sb_wspp[:], AF.Exp,
                                           scale=float(Q)), sig="s_act")
        P.op("dve", lambda e: e.memset(state_sb[:], 0.0), sig="s_dve")
        P.op("dve", lambda e: e.memset(sb_xnprev[:], 0.0), sig="s_dve")
        P.barrier()

        def emit_ln(raw_bf, mu, rstd, row_t, rsc, sq, eps):
            pe_after = {}
            for cc in range(CCn):
                pp = cc % 2
                if cc >= 2:
                    P.wait("act", "s_pe", pe_after[cc - 2])
                v = P.op("act", lambda e, cc=cc, pp=pp: e.activation(
                    sq[:, pp * 1024:(pp + 1) * 1024],
                    raw_bf[:, cc * 1024:(cc + 1) * 1024], AF.Square), sig="s_act")
                P.wait("pe", "s_act", v)
                for hh in range(2):
                    P.op("pe", MM(ps_big0[0:1, hh * 512:(hh + 1) * 512], ones_c[:],
                                  raw_bf[:, cc * 1024 + hh * 512:
                                         cc * 1024 + (hh + 1) * 512],
                                  cc == 0, cc == CCn - 1), sig="s_pe")
                    vv = P.op("pe", MM(ps_big1[0:1, hh * 512:(hh + 1) * 512],
                                       ones_c[:],
                                       sq[:, pp * 1024 + hh * 512:
                                          pp * 1024 + (hh + 1) * 512],
                                       cc == 0, cc == CCn - 1), sig="s_pe")
                pe_after[cc] = vv
            P.wait("dve", "s_pe", P.val("s_pe"))
            inv = 1.0 / C
            v = P.op("dve", lambda e: e.tensor_single_scalar(
                row_t[0:1, 0:1024], ps_big0[0:1, 0:1024], inv, ALU.mult),
                sig="s_dve")
            v = P.op("dve", lambda e: e.tensor_mul(
                rsc[0:1, 0:1024], row_t[0:1, 0:1024], row_t[0:1, 0:1024]),
                sig="s_dve")
            v = P.op("dve", lambda e: e.scalar_tensor_tensor(
                row_t[0:1, 1024:2048], ps_big1[0:1, 0:1024], inv,
                rsc[0:1, 0:1024], ALU.mult, ALU.subtract), sig="s_dve")
            P.wait("act", "s_dve", v)
            v = P.op("act", lambda e: e.activation(
                rsc[0:1, 0:1024], row_t[0:1, 1024:2048], AF.Ln, bias=float(eps)),
                sig="s_act")
            v = P.op("act", lambda e: e.activation(
                row_t[0:1, 1024:2048], rsc[0:1, 0:1024], AF.Exp, scale=-0.5),
                sig="s_act")
            P.wait("pe", "s_act", v)
            P.wait("pe", "s_dve", P.val("s_dve"))
            for hh in range(2):
                P.op("pe", MM(ps_big0[:, hh * 512:(hh + 1) * 512], ones_rf[:],
                              row_t[0:1, hh * 512:(hh + 1) * 512], True, True),
                     sig="s_pe")
                v = P.op("pe", MM(ps_big1[:, hh * 512:(hh + 1) * 512], ones_rf[:],
                                  row_t[0:1, 1024 + hh * 512:1024 + (hh + 1) * 512],
                                  True, True), sig="s_pe")
            P.wait("dve", "s_pe", v)
            v = P.op("dve", lambda e: e.tensor_copy(mu[:], ps_big0[:, 0:1024]),
                     sig="s_dve")
            v = P.op("dve", lambda e: e.tensor_copy(rstd[:], ps_big1[:, 0:1024]),
                     sig="s_dve")
            P.wait("pe", "s_dve", v)
            return v

        def emit_mix(dst, xxbuf, cc, mcol, xnb, inplace_xx=False):
            base = cc * 1024
            if not inplace_xx:
                P.op("dve", lambda e: e.tensor_sub(
                    xxbuf[:, 1:1024], xnb[:, base:base + 1023],
                    xnb[:, base + 1:base + 1024]), sig="s_dve")
                P.op("dve", lambda e: e.tensor_sub(
                    xxbuf[:, 0:1], sb_xnprev[:, cc:cc + 1],
                    xnb[:, base:base + 1]), sig="s_dve")
            return P.op("dve", lambda e: e.scalar_tensor_tensor(
                dst[:, 0:1024] if dst.shape[1] == 1024 else dst[:, base:base + 1024],
                xxbuf[:, 0:1024], mcol, xnb[:, base:base + 1024],
                ALU.mult, ALU.add), sig="s_dve")

        def emit_proj(wdram, rhs_view, evac, ringb, n_ot=CCn, wait_rhs=None):
            wsrc = wdram.rearrange("(cc p) o -> p cc o", p=128)
            evac_done = {}
            mm_last = {}
            prev_pe = P.val("s_pe")
            for ot in range(n_ot):
                slot = ot % 2
                ps = ps_big0 if ot % 2 == 0 else ps_big1
                P.wait("gp", "s_pe", mm_last[ot - 2] if ot >= 2 else prev_pe)
                vw = P.dma(
                    "gp", ringb[:].rearrange("p (s cc o) -> p s cc o", s=2, o=128)
                    [:, slot], wsrc[:, :, ot * 128:(ot + 1) * 128], "s_w")
                P.wait("pe", "s_w", vw)
                if ot >= 2:
                    eng, vv = evac_done[ot - 2]
                    P.wait("pe", "s_" + eng, vv)
                if wait_rhs is not None and ot == 0:
                    P.wait("pe", wait_rhs[0], wait_rhs[1])
                for cc in range(CCn):
                    for hh in range(2):
                        v = P.op("pe", MM(
                            ps[:, hh * 512:(hh + 1) * 512],
                            ringb[:, slot * 2048 + cc * 128:
                                  slot * 2048 + (cc + 1) * 128],
                            rhs_view(cc, hh), cc == 0, cc == CCn - 1),
                        sig="s_pe" if (cc == CCn - 1 and hh == 1) else None)
                mm_last[ot] = v
                evac_done[ot] = evac(ot, ps, v)
            return mm_last, evac_done

        def emit_ascan(kbuf, vbuf, nch, hist_buf, kwb):
            for c in range(nch):
                if hist_buf is not None:
                    v = P.op("dve", lambda e, c=c: e.tensor_copy(
                        hist_buf[:, c * 1024:(c + 1) * 1024], state_sb[:]),
                        sig="s_dve")
                for half in range(2):
                    for grp in range(2):
                        P.wait("pe", "s_dve", P.val("s_dve"))
                        for cc8 in range(8):
                            cc = grp * 8 + cc8
                            v = P.op("pe", lambda e, cc=cc, cc8=cc8, half=half, c=c:
                                     e.transpose(
                                         ps_tr[:, cc8 * 128:(cc8 + 1) * 128],
                                         kbuf[:, cc * 1024 + c * 256 + half * 128:
                                              cc * 1024 + c * 256 + half * 128 + 128],
                                         ident[:]), sig="s_pe")
                        P.wait("dve", "s_pe", v)
                        wk_ap = bass.AP(sb_wk, half * 32 + grp * 16,
                                        [[64, 128], [1, 16], [0, 64]])
                        v = P.op("dve", lambda e, half=half, grp=grp,
                                 wk_ap=wk_ap, kwb=kwb: e.tensor_tensor(
                                     kwb[:, half * 2048 + grp * 1024:
                                         half * 2048 + (grp + 1) * 1024]
                                     .rearrange("p (hh r) -> p hh r", r=64),
                                     ps_tr[:].rearrange("p (hh r) -> p hh r", r=64),
                                     wk_ap, ALU.mult), sig="s_dve")
                P.wait("pe", "s_dve", P.val("s_dve"))
                dve_state = {}
                for h in range(H):
                    rb = (h % 2) * 64
                    slot = (h % 2) * 512 + ((h // 2) % 2) * 128
                    if h >= 2:
                        P.wait("pe", "s_dve", dve_state[h - 2])
                    for ts in range(2):
                        v = P.op("pe", MM(
                            ps_big1[rb:rb + 64, slot:slot + 64],
                            kwb[:, ts * 2048 + h * 64:ts * 2048 + (h + 1) * 64],
                            vbuf[:, (c * 2 + ts) * 2048 + h * 64:
                                 (c * 2 + ts) * 2048 + (h + 1) * 64],
                            ts == 0, ts == 1, (0, rb)), sig="s_pe")
                    P.wait("dve", "s_pe", v)
                    dve_state[h] = P.op("dve", lambda e, h=h, rb=rb, slot=slot:
                                        e.scalar_tensor_tensor(
                        state_sb[rb:rb + 64, (h // 2) * 64:(h // 2) * 64 + 64],
                        state_sb[rb:rb + 64, (h // 2) * 64:(h // 2) * 64 + 64],
                        sb_wspp[rb:rb + 64, h // 2:h // 2 + 1],
                        ps_big1[rb:rb + 64, slot:slot + 64],
                        ALU.mult, ALU.add), sig="s_dve")
                P.wait("pe", "s_dve", P.val("s_dve"))

        def emit_vproj_evac(vfm_b, vtok_b):
            def evac(ot, ps, mmv):
                P.wait("dve", "s_pe", mmv)
                v1 = P.op("dve", lambda e, ps=ps: e.tensor_copy(
                    vfm_b[:], ps[:, 0:1024]), sig="s_dve")
                P.wait("pe", "s_dve", v1)
                for j in range(8):
                    vp = P.op("pe", lambda e, j=j: e.transpose(
                        ps_tr[:, j * 128:(j + 1) * 128],
                        vfm_b[:, j * 128:(j + 1) * 128], ident[:]), sig="s_pe")
                P.wait("dve", "s_pe", vp)
                v2 = P.op("dve", lambda e, ot=ot: e.tensor_copy(
                    vtok_b[:].rearrange("p (tt ch) -> p tt ch", ch=2048)
                    [:, :, ot * 128:(ot + 1) * 128],
                    ps_tr[:].rearrange("p (tt ch) -> p tt ch", ch=128)),
                    sig="s_dve")
                return ("dve", v2)
            return evac

        ckpt(PH_PRE)
        xPv = xP.rearrange("(cc p) t -> p cc t", p=128)
        for sb in range(NSB):
            P.wait("gp", "s_dve", P.val("s_dve"))
            vin = P.dma("gp", xnP[:].rearrange("p (cc t) -> p cc t", t=1024),
                        xPv[:, :, sb * 1024:(sb + 1) * 1024], "s_w")
            for e in ("pe", "act", "dve"):
                P.wait(e, "s_w", vin)
            emit_ln(xnP, mu_b, rstd_b, rows1, rowscr, sqscr, LN_EPS)
            for cc in range(CCn):
                P.op("dve", lambda e, cc=cc: e.tensor_sub(
                    xnP[:, cc * 1024:(cc + 1) * 1024],
                    xnP[:, cc * 1024:(cc + 1) * 1024], mu_b[:]), sig="s_dve")
                v = P.op("dve", lambda e, cc=cc: e.tensor_mul(
                    xnP[:, cc * 1024:(cc + 1) * 1024],
                    xnP[:, cc * 1024:(cc + 1) * 1024], rstd_b[:]), sig="s_dve")
            for cc in range(CCn):
                emit_mix(xvP, xkP[:, cc * 1024:(cc + 1) * 1024], cc,
                         sb_maa["v"][:, cc:cc + 1], xnP)
                v = emit_mix(xkP, xkP[:, cc * 1024:(cc + 1) * 1024], cc,
                             sb_maa["k"][:, cc:cc + 1], xnP, inplace_xx=True)
            v = P.op("dve", lambda e: e.tensor_copy(
                sb_xnprev[:], xnP[:].rearrange("p (cc t) -> p cc t", t=1024)
                [:, :, 1023]), sig="s_dve")
            mix_done = v
            def kevacP(ot, ps, mmv):
                P.wait("dve", "s_pe", mmv)
                v = P.op("dve", lambda e, ot=ot, ps=ps: e.tensor_copy(
                    k_fmP[:, ot * 1024:(ot + 1) * 1024], ps[:, 0:1024]),
                    sig="s_dve")
                return ("dve", v)
            emit_proj(wT["k"], lambda cc, hh: xkP[:, cc * 1024 + hh * 512:
                                                  cc * 1024 + (hh + 1) * 512],
                      kevacP, ringP, wait_rhs=("s_dve", mix_done))
            emit_proj(wT["v"], lambda cc, hh: xvP[:, cc * 1024 + hh * 512:
                                                  cc * 1024 + (hh + 1) * 512],
                      emit_vproj_evac(vfmP, v_tokP), ringP)
            emit_ascan(k_fmP, v_tokP, SBLK // Q, None, kwP)
        P.barrier()

        ckpt(PH_LN1)
        xTv = xT.rearrange("(cc p) t -> p cc t", p=128)
        P.dma("gp", x_bf[:].rearrange("p (cc t) -> p cc t", t=1024), xTv, "s_w")
        for e in ("pe", "act", "dve"):
            P.wait(e, "s_w", P.val("s_w"))
        emit_ln(x_bf, mu_b, rstd_b, rows1, rowscr, sqscr, LN_EPS)
        for qq in range(4):
            vq = P.dma("sp", xq[:].rearrange("p (cc t) -> p cc t", t=256),
                       xTv[:, :, qq * 256:(qq + 1) * 256], "s_in")
            P.wait("dve", "s_in", vq)
            for cc in range(CCn):
                P.op("dve", lambda e, cc=cc, qq=qq: e.tensor_sub(
                    xn[:, cc * 1024 + qq * 256:cc * 1024 + (qq + 1) * 256],
                    xq[:, cc * 256:(cc + 1) * 256],
                    mu_b[:, qq * 256:(qq + 1) * 256]), sig="s_dve")
                v = P.op("dve", lambda e, cc=cc, qq=qq: e.tensor_mul(
                    xn[:, cc * 1024 + qq * 256:cc * 1024 + (qq + 1) * 256],
                    xn[:, cc * 1024 + qq * 256:cc * 1024 + (qq + 1) * 256],
                    rstd_b[:, qq * 256:(qq + 1) * 256]), sig="s_dve")
            if qq < 3:
                P.wait("sp", "s_dve", v)
        P.tap("xn", xn[:])
        P.barrier()

        ckpt(PH_MIX1)
        for cc in range(CCn):
            emit_mix(xv, xx_xk[:, cc * 1024:(cc + 1) * 1024], cc,
                     sb_maa["v"][:, cc:cc + 1], xn)
            v = emit_mix(xx_xk, xx_xk[:, cc * 1024:(cc + 1) * 1024], cc,
                         sb_maa["k"][:, cc:cc + 1], xn, inplace_xx=True)
        mix_done = v

        def kevac_main(ot, ps, mmv):
            P.wait("dve", "s_pe", mmv)
            v = P.op("dve", lambda e, ot=ot: e.tensor_copy(
                k_fm[:, ot * 1024:(ot + 1) * 1024], ps[:, 0:1024]), sig="s_dve")
            return ("dve", v)

        emit_proj(wT["k"], lambda cc, hh: xx_xk[:, cc * 1024 + hh * 512:
                                                cc * 1024 + (hh + 1) * 512],
                  kevac_main, ring, wait_rhs=("s_dve", mix_done))
        emit_proj(wT["v"], lambda cc, hh: xv[:, cc * 1024 + hh * 512:
                                             cc * 1024 + (hh + 1) * 512],
                  emit_vproj_evac(vfm_tmp, v_tok), ring)
        P.tap("k_fm", k_fm[:])
        P.tap("v_tok", v_tok[:])
        emit_ascan(k_fm, v_tok, NCH, hist, kw)
        P.tap("hist", hist[:])
        P.barrier()

        ckpt(PH_MIX2)
        for cc in range(CCn):
            emit_mix(xr, xx_scr, cc, sb_maa["r"][:, cc:cc + 1], xn)
            v = emit_mix(xn, xx_scr, cc, sb_maa["g"][:, cc:cc + 1], xn,
                         inplace_xx=True)
        mix2_done = v

        def revac(ot, ps, mmv):
            P.wait("dve", "s_pe", mmv)
            v = P.op("dve", lambda e, ot=ot, ps=ps: e.tensor_copy(
                r_fm[:, ot * 1024:(ot + 1) * 1024], ps[:, 0:1024]), sig="s_dve")
            return ("dve", v)

        emit_proj(wT["r"], lambda cc, hh: xr[:, cc * 1024 + hh * 512:
                                             cc * 1024 + (hh + 1) * 512],
                  revac, ring, wait_rhs=("s_dve", mix2_done))

        def gevac(ot, ps, mmv):
            P.wait("act", "s_pe", mmv)
            v = P.op("act", lambda e, ot=ot, ps=ps: e.activation(
                g_fm[:, ot * 1024:(ot + 1) * 1024], ps[:, 0:1024], AF.Silu),
                sig="s_act")
            return ("act", v)

        emit_proj(wT["g"], lambda cc, hh: xn[:, cc * 1024 + hh * 512:
                                             cc * 1024 + (hh + 1) * 512],
                  gevac, ring)
        P.tap("r_fm", r_fm[:])
        P.tap("g_fm", g_fm[:])
        P.barrier()

        ckpt(PH_Y)
        for dst, key in [(sb_dpos, "DPOS"), (sb_mlow, "MLOW"), (sb_diag, "DIAG"),
                         (sb_qmodb, "QMODB")]:
            P.dma("sp", dst[:], conh[key].ap(), "s_in")
        for e in ("dve", "act", "pe"):
            P.wait(e, "s_in", P.val("s_in"))

        NGRP = H // 4
        unit = 0
        yn_done, gn_done, tr_done, z_done = {}, {}, {}, {}
        am_slot_guard = [0, 0]
        for grp in range(NGRP):
            heads = [grp * 4 + j for j in range(4)]
            ccs = [grp * 2, grp * 2 + 1]
            P.wait("dve", "s_act", P.val("s_act"))
            for j, h in enumerate(heads):
                v = P.op("dve", lambda e, h=h: e.tensor_single_scalar(
                    wmtmp[:], sb_dpos[:], sb_lnwhb[:, h:h + 1], ALU.mult),
                    sig="s_dve")
                P.wait("act", "s_dve", v)
                v = P.op("act", lambda e: e.activation(wmtmp[:], wmtmp[:], AF.Exp),
                         sig="s_act")
                P.wait("dve", "s_act", v)
                v = P.op("dve", lambda e: e.tensor_mul(wmtmp[:], wmtmp[:],
                                                       sb_mlow[:]), sig="s_dve")
                v = P.op("dve", lambda e, h=h, j=j: e.scalar_tensor_tensor(
                    wmat[:, j * 512:(j + 1) * 512], sb_diag[:],
                    sb_uhb[:, h:h + 1], wmtmp[:], ALU.mult, ALU.add), sig="s_dve")
            for jc, cc in enumerate(ccs):
                v = P.op("dve", lambda e, cc=cc: e.tensor_single_scalar(
                    wbtmp[:], sb_qmodb[:], sb_lnwpp[:, cc:cc + 1], ALU.mult),
                    sig="s_dve")
                P.wait("act", "s_dve", v)
                v = P.op("act", lambda e: e.activation(wbtmp[:], wbtmp[:], AF.Exp),
                         sig="s_act")
                P.wait("dve", "s_act", v)
                v = P.op("dve", lambda e, jc=jc, cc=cc: e.tensor_mul(
                    rwb[:, jc * 1024:(jc + 1) * 1024],
                    r_fm[:, cc * 1024:(cc + 1) * 1024], wbtmp[:]), sig="s_dve")
            wm_ready = P.val("s_dve")
            P.wait("pe", "s_dve", wm_ready)

            for c in range(NCH):
                am_done = {}
                if Y_STAGE < 2:
                    continue
                for j, h in enumerate(heads):
                    cc = ccs[j // 2]
                    rb = (h % 2) * 64
                    sl = (j % 2) * 512
                    ccol = cc * 1024 + c * 256
                    if am_slot_guard[j % 2] > 0:
                        P.wait("pe", "s_dve", am_slot_guard[j % 2])
                    for half in range(2):
                        v = P.op("pe", MM(
                            ps_big0[:, sl + half * 256:sl + (half + 1) * 256],
                            k_fm[rb:rb + 64, ccol + half * 128:ccol + half * 128 + 128],
                            r_fm[rb:rb + 64, ccol:ccol + 256],
                            True, True), sig="s_pe")
                    P.wait("dve", "s_pe", v)
                    am_done[j] = P.op("dve", lambda e, j=j, sl=sl: e.tensor_mul(
                        am[:, j * 512:(j + 1) * 512], ps_big0[:, sl:sl + 512],
                        wmat[:, j * 512:(j + 1) * 512]), sig="s_dve")
                    am_slot_guard[j % 2] = am_done[j]
                if Y_STAGE < 3:
                    continue
                for qs in range(2):
                    ysl = (unit % 2) * 512
                    if unit >= 2 and (unit - 2) in gn_done:
                        P.wait("pe", "s_dve", gn_done[unit - 2])
                    for j, h in enumerate(heads):
                        cc = ccs[j // 2]
                        rb = (h % 2) * 64
                        P.wait("pe", "s_dve", am_done[j])
                        v = P.op("pe", MM(
                            ps_big1[:, ysl + j * 64:ysl + (j + 1) * 64],
                            am[:, j * 512 + 0 * 256 + qs * 128:
                               j * 512 + 0 * 256 + qs * 128 + 128],
                            v_tok[:, (c * 2 + 0) * 2048 + h * 64:
                                  (c * 2 + 0) * 2048 + (h + 1) * 64],
                            True, False), sig="s_pe")
                        v = P.op("pe", MM(
                            ps_big1[:, ysl + j * 64:ysl + (j + 1) * 64],
                            am[:, j * 512 + 1 * 256 + qs * 128:
                               j * 512 + 1 * 256 + qs * 128 + 128],
                            v_tok[:, (c * 2 + 1) * 2048 + h * 64:
                                  (c * 2 + 1) * 2048 + (h + 1) * 64],
                            False, False), sig="s_pe")
                        v = P.op("pe", MM(
                            ps_big1[:, ysl + j * 64:ysl + (j + 1) * 64],
                            rwb[rb:rb + 64, (j // 2) * 1024 + c * 256 + qs * 128:
                                (j // 2) * 1024 + c * 256 + qs * 128 + 128],
                            hist[rb:rb + 64, c * 1024 + (h // 2) * 64:
                                 c * 1024 + (h // 2) * 64 + 64],
                            False, True), sig="s_pe")
                    ymm = v
                    if Y_STAGE < 4:
                        unit += 1
                        continue
                    up = (unit % 2)
                    so = up * 20
                    yo = up * 512
                    if unit >= 2 and (unit - 2) in yn_done:
                        P.wait("dve", "s_act", yn_done[unit - 2])
                    P.wait("dve", "s_pe", ymm)
                    ycp = P.op("dve", lambda e, ysl=ysl, yo=yo: e.tensor_copy(
                        ytile[:, yo:yo + 256], ps_big1[:, ysl:ysl + 256]),
                        sig="s_dve")
                    gn_done[unit] = ycp
                    P.selfwait("dve")
                    P.wait("act", "s_dve", ycp)
                    sq_v = P.op("act", lambda e, yo=yo: e.activation(
                        ytile[:, yo + 256:yo + 512], ytile[:, yo:yo + 256],
                        AF.Square), sig="s_act")
                    for j in range(4):
                        P.op("dve", lambda e, j=j, yo=yo, so=so: e.reduce_sum(
                            stat8[:, so + j:so + j + 1],
                            ytile[:, yo + j * 64:yo + (j + 1) * 64],
                            mybir.AxisListType.X), sig="s_dve")
                    P.wait("dve", "s_act", sq_v)
                    for j in range(4):
                        P.op("dve", lambda e, j=j, yo=yo, so=so: e.reduce_sum(
                            stat8[:, so + 4 + j:so + 5 + j],
                            ytile[:, yo + 256 + j * 64:yo + 256 + (j + 1) * 64],
                            mybir.AxisListType.X), sig="s_dve")
                    if Y_STAGE < 5:
                        yn_done[unit] = P.val("s_act")
                        unit += 1
                        continue
                    P.selfwait("dve")
                    P.op("dve", lambda e, so=so: e.tensor_single_scalar(
                        stat8[:, so + 8:so + 12], stat8[:, so:so + 4],
                        1.0 / HD, ALU.mult), sig="s_dve")
                    P.selfwait("dve")
                    P.op("dve", lambda e, so=so: e.tensor_mul(
                        stat8[:, so + 16:so + 20], stat8[:, so + 8:so + 12],
                        stat8[:, so + 8:so + 12]), sig="s_dve")
                    P.selfwait("dve")
                    v = P.op("dve", lambda e, so=so: e.scalar_tensor_tensor(
                        stat8[:, so + 12:so + 16], stat8[:, so + 4:so + 8],
                        1.0 / HD, stat8[:, so + 16:so + 20], ALU.mult,
                        ALU.subtract), sig="s_dve")
                    if Y_STAGE < 4.9:
                        yn_done[unit] = P.val("s_act")
                        unit += 1
                        continue
                    P.wait("act", "s_dve", v)
                    v = P.op("act", lambda e, so=so: e.activation(
                        stat8[:, so + 16:so + 20], stat8[:, so + 12:so + 16],
                        AF.Ln, bias=float(GN_EPS)), sig="s_act")
                    P.selfwait("act")
                    v = P.op("act", lambda e, so=so: e.activation(
                        stat8[:, so + 12:so + 16], stat8[:, so + 16:so + 20],
                        AF.Exp, scale=-0.5), sig="s_act")
                    P.wait("dve", "s_act", v)
                    gnv = P.op("dve", lambda e, so=so: e.scalar_tensor_tensor(
                        stat8[:, so + 16:so + 20], stat8[:, so + 8:so + 12],
                        -1.0, stat8[:, so + 12:so + 16], ALU.mult, ALU.mult),
                        sig="s_dve")
                    P.wait("act", "s_dve", gnv)
                    if Y_STAGE < 6:
                        yn_done[unit] = P.val("s_act")
                        unit += 1
                        continue
                    tpp = unit % 2
                    if unit >= 2 and (unit - 2) in tr_done:
                        P.wait("act", "s_pe", tr_done[unit - 2])
                    for j in range(4):
                        v = P.op("act", lambda e, j=j, yo=yo, so=so, tpp=tpp:
                                 e.activation(
                            ynst[:, tpp * 256 + j * 64:tpp * 256 + (j + 1) * 64],
                            ytile[:, yo + j * 64:yo + (j + 1) * 64],
                            AF.Identity, scale=stat8[:, so + 12 + j:so + 13 + j],
                            bias=stat8[:, so + 16 + j:so + 17 + j]), sig="s_act")
                    yn_done[unit] = v
                    if Y_STAGE < 7:
                        unit += 1
                        continue
                    P.wait("pe", "s_act", v)
                    if unit >= 1 and (unit - 1) in z_done:
                        P.wait("pe", "s_dve", z_done[unit - 1])
                    for t2 in range(2):
                        v = P.op("pe", lambda e, t2=t2, tpp=tpp: e.transpose(
                            ps_tr[:, tpp * 256 + t2 * 128:tpp * 256 + t2 * 128 + 128],
                            ynst[:, tpp * 256 + t2 * 128:tpp * 256 + t2 * 128 + 128],
                            ident[:]), sig="s_pe")
                    tr_done[unit] = v
                    P.wait("dve", "s_pe", v)
                    for t2 in range(2):
                        cc = ccs[t2]
                        zcol = cc * 1024 + c * 256 + qs * 128
                        P.op("dve", lambda e, t2=t2, cc=cc, tpp=tpp: e.tensor_scalar(
                            wbtmp[:, 256 + t2 * 128:256 + (t2 + 1) * 128],
                            ps_tr[:, tpp * 256 + t2 * 128:tpp * 256 + t2 * 128 + 128],
                            sb_gamma[:, cc:cc + 1], sb_beta[:, cc:cc + 1],
                            ALU.mult, ALU.add), sig="s_dve")
                        P.selfwait("dve")
                        v = P.op("dve", lambda e, t2=t2, zcol=zcol, cc=cc:
                                 e.tensor_mul(
                            z_fm[:, zcol:zcol + 128],
                            wbtmp[:, 256 + t2 * 128:256 + (t2 + 1) * 128],
                            g_fm[:, zcol:zcol + 128]), sig="s_dve")
                    z_done[unit] = v
                    unit += 1
        P.tap("z_fm", z_fm[:])
        P.tap("wmat", wmat[:])
        P.tap("rwb", rwb[:])
        P.tap("am", am[:])
        P.tap("ytile", ytile[:])
        P.tap("stat8", stat8[:])
        P.tap("ynst", ynst[:])
        P.barrier()

        ckpt(PH_WO)
        vx = P.dma("sp", x2[:].rearrange("p (cc t) -> p cc t", t=1024), xTv, "s_in")
        P.wait("dve", "s_in", vx)
        wsrc_o = wT["o"].rearrange("(cc p) o -> p cc o", p=128)
        mm_last_o = {}
        ev_o = {}
        for ot in range(CCn):
            slot = ot % 2
            ps = ps_big0 if ot % 2 == 0 else ps_big1
            if ot >= 2:
                P.wait("gp", "s_pe", mm_last_o[ot - 2])
            vw = P.dma("gp", ring2[:].rearrange("p (s cc o) -> p s cc o", s=2, o=128)
                       [:, slot], wsrc_o[:, :, ot * 128:(ot + 1) * 128], "s_w")
            P.wait("pe", "s_w", vw)
            if ot >= 2:
                P.wait("pe", "s_dve", ev_o[ot - 2])
            for cc in range(CCn):
                for hh in range(2):
                    v = P.op("pe", MM(
                        ps[:, hh * 512:(hh + 1) * 512],
                        ring2[:, slot * 2048 + cc * 128:slot * 2048 + (cc + 1) * 128],
                        z_fm[:, cc * 1024 + hh * 512:cc * 1024 + (hh + 1) * 512],
                        cc == 0, cc == CCn - 1), sig="s_pe")
            mm_last_o[ot] = v
            P.wait("dve", "s_pe", v)
            ev_o[ot] = P.op("dve", lambda e, ot=ot, ps=ps: e.tensor_add(
                x2[:, ot * 1024:(ot + 1) * 1024],
                x2[:, ot * 1024:(ot + 1) * 1024], ps[:, 0:1024]), sig="s_dve")
        P.tap("x2", x2[:])
        P.barrier()

        ckpt(PH_LN2)
        for cc in range(CCn):
            v = P.op("dve", lambda e, cc=cc: e.tensor_copy(
                x2_bf[:, cc * 1024:(cc + 1) * 1024],
                x2[:, cc * 1024:(cc + 1) * 1024]), sig="s_dve")
        for e in ("pe", "act"):
            P.wait(e, "s_dve", v)
        emit_ln(x2_bf, mu2_b, rstd2_b, rows2, rowscr2, sqscr2, LN_EPS)
        for cc in range(CCn):
            P.op("dve", lambda e, cc=cc: e.tensor_sub(
                h_fm[:, cc * 1024:(cc + 1) * 1024],
                x2[:, cc * 1024:(cc + 1) * 1024], mu2_b[:]), sig="s_dve")
            v = P.op("dve", lambda e, cc=cc: e.tensor_mul(
                h_fm[:, cc * 1024:(cc + 1) * 1024],
                h_fm[:, cc * 1024:(cc + 1) * 1024], rstd2_b[:]), sig="s_dve")
        P.tap("h_fm", h_fm[:])
        P.barrier()

        ckpt(PH_FFN)
        for cc in range(CCn):
            for hh in range(2):
                v = P.op("pe", MM(ps_big0[0:1, hh * 512:(hh + 1) * 512],
                                  ones_c[:], h_fm[:, cc * 1024 + hh * 512:
                                                  cc * 1024 + (hh + 1) * 512],
                                  cc == 0, cc == CCn - 1), sig="s_pe")
        P.wait("dve", "s_pe", v)
        v = P.op("dve", lambda e: e.tensor_single_scalar(
            cneg_bf[0:1, 0:1024], ps_big0[0:1, 0:1024], -128.0, ALU.mult),
            sig="s_dve")
        P.wait("pe", "s_dve", v)

        gsrc = gateT.rearrange("(cc p) i -> p cc i", p=128)
        usrc = upT.rearrange("(cc p) i -> p cc i", p=128)
        gu_last = {}
        sg_done = {}
        f_done = {}
        dn_last = {}
        dn_evac = {}
        prev_pe0 = P.val("s_pe")
        for qq in range(4):
            for itl in range(16):
                it = qq * 16 + itl
                gslot = (2 * it) % 4
                uslot = (2 * it + 1) % 4
                P.wait("gp", "s_pe", gu_last[it - 2] if it >= 2 else prev_pe0)
                vg = P.dma("gp", ring_gu[:].rearrange(
                    "p (s cc o) -> p s cc o", s=4, o=128)[:, gslot],
                    gsrc[:, :, it * 128:(it + 1) * 128], "s_w")
                vu = P.dma("gp", ring_gu[:].rearrange(
                    "p (s cc o) -> p s cc o", s=4, o=128)[:, uslot],
                    usrc[:, :, it * 128:(it + 1) * 128], "s_w")
                if it >= 1:
                    P.wait("pe", "s_act", sg_done[it - 1])
                    P.wait("pe", "s_dve", f_done[it - 1])
                P.wait("pe", "s_w", vg)
                for hh in range(2):
                    P.op("pe", MM(ps_big0[:, hh * 512:(hh + 1) * 512], ones_rb[:],
                                  cneg_bf[0:1, hh * 512:(hh + 1) * 512],
                                  True, False), sig="s_pe")
                    for cc in range(CCn):
                        P.op("pe", MM(
                            ps_big0[:, hh * 512:(hh + 1) * 512],
                            ring_gu[:, gslot * 2048 + cc * 128:
                                    gslot * 2048 + (cc + 1) * 128],
                            h_fm[:, cc * 1024 + hh * 512:cc * 1024 + (hh + 1) * 512],
                            False, cc == CCn - 1), sig="s_pe")
                P.wait("pe", "s_w", vu)
                for hh in range(2):
                    P.op("pe", MM(ps_big1[:, hh * 512:(hh + 1) * 512], ones_rb[:],
                                  cneg_bf[0:1, hh * 512:(hh + 1) * 512],
                                  True, False), sig="s_pe")
                    for cc in range(CCn):
                        v = P.op("pe", MM(
                            ps_big1[:, hh * 512:(hh + 1) * 512],
                            ring_gu[:, uslot * 2048 + cc * 128:
                                    uslot * 2048 + (cc + 1) * 128],
                            h_fm[:, cc * 1024 + hh * 512:cc * 1024 + (hh + 1) * 512],
                            False, cc == CCn - 1), sig="s_pe")
                gu_last[it] = v
                P.wait("act", "s_pe", v)
                sg_done[it] = P.op("act", lambda e, it=it: e.activation(
                    sg_tmp[:], ps_big0[:, 0:1024], AF.Silu,
                    scale=sb_gs[:, it:it + 1]), sig="s_act")
                P.wait("dve", "s_act", sg_done[it])
                P.op("dve", lambda e: e.tensor_mul(
                    f0_tmp[:], sg_tmp[:], ps_big1[:, 0:1024]), sig="s_dve")
                f_done[it] = P.op("dve", lambda e, it=it, itl=itl:
                                  e.tensor_single_scalar(
                    f_sb[:, itl * 1024:(itl + 1) * 1024], f0_tmp[:],
                    sb_us[:, it:it + 1], ALU.mult), sig="s_dve")
            P.wait("pe", "s_dve", f_done[qq * 16 + 15])
            for ic in range(CCn):
                for hh in range(2):
                    v = P.op("pe", MM(ps_big0[0:1, hh * 512:(hh + 1) * 512],
                                      ones_c[:],
                                      f_sb[:, ic * 1024 + hh * 512:
                                           ic * 1024 + (hh + 1) * 512],
                                      ic == 0, ic == CCn - 1), sig="s_pe")
            P.wait("dve", "s_pe", v)
            v = P.op("dve", lambda e: e.tensor_single_scalar(
                cb_rows[0:1, 0:1024], ps_big0[0:1, 0:1024], -128.0, ALU.mult),
                sig="s_dve")
            P.selfwait("dve")
            v = P.op("dve", lambda e: e.tensor_copy(
                cneg_bf[0:1, 1024:2048], cb_rows[0:1, 0:1024]), sig="s_dve")
            P.selfwait("dve")
            v = P.op("dve", lambda e: e.tensor_sub(
                cb_rows[0:1, 1024:2048], cb_rows[0:1, 0:1024],
                cneg_bf[0:1, 1024:2048]), sig="s_dve")
            P.selfwait("dve")
            v = P.op("dve", lambda e: e.tensor_copy(
                cneg_bf[0:1, 2048:3072], cb_rows[0:1, 1024:2048]), sig="s_dve")
            P.wait("pe", "s_dve", v)
            dsrc = downT[qq * 2048:(qq + 1) * 2048, :].rearrange(
                "(ic p) hcol -> p ic hcol", p=128)
            prev_pe1 = P.val("s_pe")
            for ht in range(CCn):
                dslot = ht % 4
                P.wait("gp", "s_pe", dn_last[(qq, ht - 4)] if ht >= 4 else prev_pe1)
                vw = P.dma("gp", ring_dn[:].rearrange(
                    "p (s ic hcol) -> p s ic hcol", s=4, hcol=128)[:, dslot],
                    dsrc[:, :, ht * 128:(ht + 1) * 128], "s_w")
                P.wait("pe", "s_w", vw)
                if ht >= 1:
                    P.wait("pe", "s_dve", dn_evac[(qq, ht - 1)])
                for hh in range(2):
                    P.op("pe", MM(ps_state[:, hh * 512:(hh + 1) * 512], ones_rb[:],
                                  cneg_bf[0:1, 1024 + hh * 512:1024 + (hh + 1) * 512],
                                  True, False), sig="s_pe")
                    P.op("pe", MM(ps_state[:, hh * 512:(hh + 1) * 512], ones_rb[:],
                                  cneg_bf[0:1, 2048 + hh * 512:2048 + (hh + 1) * 512],
                                  False, False), sig="s_pe")
                    for ic in range(CCn):
                        v = P.op("pe", MM(
                            ps_state[:, hh * 512:(hh + 1) * 512],
                            ring_dn[:, dslot * 2048 + ic * 128:
                                    dslot * 2048 + (ic + 1) * 128],
                            f_sb[:, ic * 1024 + hh * 512:ic * 1024 + (hh + 1) * 512],
                            False, ic == CCn - 1), sig="s_pe")
                dn_last[(qq, ht)] = v
                P.wait("dve", "s_pe", v)
                dn_evac[(qq, ht)] = P.op("dve", lambda e, ht=ht:
                                         e.scalar_tensor_tensor(
                    x2[:, ht * 1024:(ht + 1) * 1024], ps_state[:, 0:1024],
                    sb_ds[:, ht:ht + 1], x2[:, ht * 1024:(ht + 1) * 1024],
                    ALU.mult, ALU.add), sig="s_dve")
        P.barrier()

        vo = P.dma("sp", out_hbm.rearrange("(cc p) t -> p cc t", p=128),
                   x2[:].rearrange("p (cc t) -> p cc t", t=1024), "s_out")
        P.wait("sp", "s_out", P.val("s_out"))


    except _PhaseStop:
        P.barrier()
        v = P.op("dve", lambda e: e.memset(x2[:], 0.0), sig="s_dve")
        P.wait("sp", "s_dve", v)
        vo = P.dma("sp", out_hbm.rearrange("(cc p) t -> p cc t", p=128),
                   x2[:].rearrange("p (cc t) -> p cc t", t=1024), "s_out")
        P.wait("sp", "s_out", P.val("s_out"))

    with nc.Block() as block:
        @block.tensor
        def _(e):
            for fn in P.stream["pe"]:
                fn(e)

        @block.vector
        def _(e):
            for fn in P.stream["dve"]:
                fn(e)

        @block.scalar
        def _(e):
            for fn in P.stream["act"]:
                fn(e)

        @block.gpsimd
        def _(e):
            for fn in P.stream["gp"]:
                fn(e)

        @block.sync
        def _(e):
            for fn in P.stream["sp"]:
                fn(e)

    return nc, P


_PROG_CACHE = {}


def _get_prog():
    key = tuple(DEBUG_TAPS)
    if key not in _PROG_CACHE:
        _PROG_CACHE[key] = build_program()
    return _PROG_CACHE[key]


def _pp(vec, ncols):
    return np.ascontiguousarray(
        np.asarray(vec, np.float32).reshape(-1).reshape(ncols, 128).T)


def kernel(L0_attn_x, L0_attn_time_maa_k, L0_attn_time_maa_v,
           L0_attn_time_maa_r, L0_attn_time_maa_g, L0_attn_time_decay,
           L0_attn_time_faaaa, L0_attn_Wr, L0_attn_Wk, L0_attn_Wv,
           L0_attn_Wg, L0_attn_Wo, L0_attn_gamma, L0_attn_beta,
           L0_ffn_gate_w, L0_ffn_gate_scale, L0_ffn_up_w, L0_ffn_up_scale,
           L0_ffn_down_w, L0_ffn_down_scale):
    nc, P = _get_prog()
    x = np.asarray(L0_attn_x, np.float32)

    shared = {
        "wrT": np.ascontiguousarray(np.asarray(L0_attn_Wr, np.float32).T),
        "wkT": np.ascontiguousarray(np.asarray(L0_attn_Wk, np.float32).T),
        "wvT": np.ascontiguousarray(np.asarray(L0_attn_Wv, np.float32).T),
        "wgT": np.ascontiguousarray(np.asarray(L0_attn_Wg, np.float32).T),
        "woT": np.ascontiguousarray(np.asarray(L0_attn_Wo, np.float32).T),
        "gateT": np.ascontiguousarray(np.asarray(L0_ffn_gate_w, np.int32).T),
        "upT": np.ascontiguousarray(np.asarray(L0_ffn_up_w, np.int32).T),
        "downT": np.ascontiguousarray(np.asarray(L0_ffn_down_w, np.int32).T),
        "maa_k": _pp(L0_attn_time_maa_k, 16),
        "maa_v": _pp(L0_attn_time_maa_v, 16),
        "maa_r": _pp(L0_attn_time_maa_r, 16),
        "maa_g": _pp(L0_attn_time_maa_g, 16),
        "gamma_pp": _pp(L0_attn_gamma, 16),
        "beta_pp": _pp(L0_attn_beta, 16),
        "gs_pp": _pp(L0_ffn_gate_scale, 64),
        "us_pp": _pp(L0_ffn_up_scale, 64),
        "ds_pp": _pp(L0_ffn_down_scale, 16),
        "decay": np.ascontiguousarray(
            np.asarray(L0_attn_time_decay, np.float32).reshape(32, 1)),
        "faaaa": np.ascontiguousarray(
            np.asarray(L0_attn_time_faaaa, np.float32).reshape(32, 1)),
    }

    in_maps = []
    for core in range(8):
        b, qq = core // 4, core % 4
        t0 = qq * NT
        m = dict(shared)
        m["xT"] = np.ascontiguousarray(x[b, t0:t0 + NT].T)
        win = np.zeros((PREF, C), np.float32)
        if t0 > 0:
            win[PREF - t0:] = x[b, 0:t0]
        m["xP"] = np.ascontiguousarray(win.T)
        in_maps.append(m)

    res = run_bass_kernel_spmd(nc, in_maps, core_ids=list(range(8)))

    full = np.empty((B, T, C), np.float32)
    for core in range(8):
        b, qq = core // 4, core % 4
        full[b, qq * NT:(qq + 1) * NT] = res.results[core]["out_fm"].T
    if DEBUG_TAPS:
        kernel.last_results = res
    return full
